# version 01
# baseline (speedup 1.0000x reference)
import numpy as np
import concourse.bass as bass
import concourse.mybir as mybir
from concourse.bass import IndirectOffsetOnAxis

F32 = mybir.dt.float32
BF16 = mybir.dt.bfloat16
U32 = mybir.dt.uint32
AF = mybir.ActivationFunctionType
ALU = mybir.AluOpType
AX = mybir.AxisListType

NCORE = 8
D = 1024
NTOK = 32768
TPC = 4096
NT = TPC // 128
NCH = D // 128
E = 8
CAP = 8192
NSLOT = E * CAP
SPC = NSLOT // NCORE
NST = SPC // 128
OOB = float(1 << 26)
EPS_LN = 1e-5
EPS_ATTN = 1e-8
RG = [[0, 1, 2, 3, 4, 5, 6, 7]]


def view3(t, p_cnt, g, inner, offset=0, gstride=None):
    a = t.ap()
    row = a.ap[0][0]
    gs = inner if gstride is None else gstride
    return bass.AP(a.tensor, a.offset + offset, [[row, p_cnt], [gs, g], [1, inner]])


def bcast3(t, p_cnt, g, inner, offset=0):
    a = t.ap()
    row = a.ap[0][0]
    return bass.AP(a.tensor, a.offset + offset, [[row, p_cnt], [1, g], [0, inner]])


def bcast_mid(t, p_cnt, g, mid, inner, offset=0):
    a = t.ap()
    row = a.ap[0][0]
    return bass.AP(a.tensor, a.offset + offset,
                   [[row, p_cnt], [inner, g], [0, mid], [1, inner]])


def make_const_inputs(core):
    c = {}
    c["ident_i"] = np.eye(128, dtype=np.float32)
    l = np.zeros((128, 128), np.float32)
    for k in range(128):
        l[k, k + 1:] = 1.0
    c["lstrict_i"] = l
    c["iota8_i"] = np.tile(np.arange(8, dtype=np.float32), (128, 1))
    c["ones1_i"] = np.ones((1, 128), np.float32)
    c["onesc_i"] = np.ones((128, 1), np.float32)
    bm = np.zeros((128, 16), np.float32)
    for r in range(NCORE):
        for e in range(8):
            if r < core:
                bm[r * 16 + e, e] = 1.0
                bm[r * 16 + 8 + e, e + 8] = 1.0
            bm[r * 16 + e, e + 8] += 1.0
    c["basemat_i"] = bm
    p = np.arange(128).reshape(128, 1)
    t = np.arange(NT).reshape(1, NT)
    j0 = 2.0 * (core * TPC + t * 128 + p)
    ja = np.zeros((128, NT * 2), np.float32)
    ja[:, 0::2] = j0
    ja[:, 1::2] = j0 + 1
    c["jall_i"] = ja
    st = np.arange(NST).reshape(1, NST)
    c["gidx_i"] = (core * SPC + st * 128 + p).astype(np.uint32)
    c["eps_i"] = np.full((128, 1), EPS_LN, np.float32)
    return c


def build(stage="A"):
    nc = bass.Bass("TRN2", target_bir_lowering=False)

    xf = nc.dram_tensor("xf", [NTOK, D], F32, kind="ExternalInput")
    xm = nc.dram_tensor("xm", [TPC, D], F32, kind="ExternalInput")
    wg = nc.dram_tensor("wg", [128, NCH * 8], F32, kind="ExternalInput")
    ident_i = nc.dram_tensor("ident_i", [128, 128], F32, kind="ExternalInput")
    lstrict_i = nc.dram_tensor("lstrict_i", [128, 128], F32, kind="ExternalInput")
    iota8_i = nc.dram_tensor("iota8_i", [128, 8], F32, kind="ExternalInput")
    ones1_i = nc.dram_tensor("ones1_i", [1, 128], F32, kind="ExternalInput")
    onesc_i = nc.dram_tensor("onesc_i", [128, 1], F32, kind="ExternalInput")
    basemat_i = nc.dram_tensor("basemat_i", [128, 16], F32, kind="ExternalInput")
    jall_i = nc.dram_tensor("jall_i", [128, NT * 2], F32, kind="ExternalInput")
    gidx_i = nc.dram_tensor("gidx_i", [128, NST], U32, kind="ExternalInput")
    eps_i = nc.dram_tensor("eps_i", [128, 1], F32, kind="ExternalInput")

    dbg_rt = nc.dram_tensor("dbg_rt", [128, 8 * NT], F32, kind="ExternalOutput")
    dbg_rec = nc.dram_tensor("dbg_rec", [128, NST * 4], F32, kind="ExternalOutput")

    cnt_in = nc.dram_tensor("cnt_in", [16, 1], F32)
    cnt_out = nc.dram_tensor("cnt_out", [128, 1], F32)
    rec_c = nc.dram_tensor("rec_c", [NSLOT, 4], F32)
    rec_all = nc.dram_tensor("rec_all", [NSLOT, 4], F32)
    runrow_d = nc.dram_tensor("runrow_d", [1, NT * 16], F32)
    ew1 = nc.dram_tensor("ew1", [128, NCH * 4096], F32, kind="ExternalInput")
    ew2 = nc.dram_tensor("ew2", [128, 32 * D], F32, kind="ExternalInput")
    eb1_i = nc.dram_tensor("eb1_i", [128, 32], F32, kind="ExternalInput")
    eb2_i = nc.dram_tensor("eb2_i", [128, 8], F32, kind="ExternalInput")
    identb_i = nc.dram_tensor("identb_i", [128, 128], BF16, kind="ExternalInput")
    wqkv_l = nc.dram_tensor("wqkv_l", [128, 8 * 24 * 128], F32, kind="ExternalInput")
    proj_l = nc.dram_tensor("proj_l", [64, 16 * 256], F32, kind="ExternalInput")
    wout_l = nc.dram_tensor("wout_l", [128, 8 * 8 * 128], F32, kind="ExternalInput")
    w1a_l = nc.dram_tensor("w1a_l", [128, 32 * 8 * 128], F32, kind="ExternalInput")
    w2a_l = nc.dram_tensor("w2a_l", [128, 32 * 8 * 128], F32, kind="ExternalInput")
    bout_c = nc.dram_tensor("bout_c", [128, 8], F32, kind="ExternalInput")
    gam_c = nc.dram_tensor("gam_c", [128, 8], F32, kind="ExternalInput")
    bet_c = nc.dram_tensor("bet_c", [128, 8], F32, kind="ExternalInput")
    b1a_c = nc.dram_tensor("b1a_c", [128, 32], F32, kind="ExternalInput")
    b2a_c = nc.dram_tensor("b2a_c", [128, 8], F32, kind="ExternalInput")
    wq_bf = nc.dram_tensor("wq_bf", [128, 8 * 8 * 128], BF16)
    wout_bf = nc.dram_tensor("wout_bf", [128, 8 * 8 * 128], BF16)
    w1a_bf = nc.dram_tensor("w1a_bf", [128, 32 * 8 * 128], BF16)
    w2a_bf = nc.dram_tensor("w2a_bf", [128, 32 * 8 * 128], BF16)
    a2_ao = nc.dram_tensor("a2_ao", [128, 4096], BF16)
    a2_x3 = nc.dram_tensor("a2_x3", [128, 4096], F32)
    a2_x4 = nc.dram_tensor("a2_x4", [128, 4096], F32)
    a2_xnq = nc.dram_tensor("a2_xnq", [128, 4096], BF16)
    a2_qf = nc.dram_tensor("a2_qf", [128, 1024], BF16)
    kv_in = nc.dram_tensor("kv_in", [128, 32 * 65], F32)
    kv_out = nc.dram_tensor("kv_out", [128, 32 * 65], F32)
    out_d = nc.dram_tensor("out_d", [TPC, D], F32, kind="ExternalOutput")
    RGP = [[0, 1], [2, 3], [4, 5], [6, 7]]
    y_mine = nc.dram_tensor("y_mine", [SPC, D], BF16)
    y_all = nc.dram_tensor("y_all", [NSLOT, D], BF16)

    from contextlib import ExitStack
    es = ExitStack()
    ident = es.enter_context(nc.sbuf_tensor("ident", [128, 128], F32))
    lstrict = es.enter_context(nc.sbuf_tensor("lstrict", [128, 128], F32))
    iota8 = es.enter_context(nc.sbuf_tensor("iota8", [128, 8], F32))
    ones1 = es.enter_context(nc.sbuf_tensor("ones1", [1, 128], F32))
    onesc = es.enter_context(nc.sbuf_tensor("onesc", [128, 1], F32))
    basemat = es.enter_context(nc.sbuf_tensor("basemat", [128, 16], F32))
    jall = es.enter_context(nc.sbuf_tensor("jall", [128, NT * 2], F32))
    gidxc = es.enter_context(nc.sbuf_tensor("gidxc", [128, NST], U32))
    epsc = es.enter_context(nc.sbuf_tensor("epsc", [128, 1], F32))
    wg_sb = es.enter_context(nc.sbuf_tensor("wg_sb", [128, NCH * 8], F32))
    xt = es.enter_context(nc.sbuf_tensor("xt", [128, 2 * D], F32))
    xn_tm = es.enter_context(nc.sbuf_tensor("xn_tm", [128, 2 * D], F32))
    xn_f32 = es.enter_context(nc.sbuf_tensor("xn_f32", [128, 2 * D], F32))
    ew1_sb = es.enter_context(nc.sbuf_tensor("ew1_sb", [128, NCH * 4096], BF16))
    xn_fm = ew1_sb
    ew2_sb = es.enter_context(nc.sbuf_tensor("ew2_sb", [128, 32 * D], BF16))
    eb1_sb = es.enter_context(nc.sbuf_tensor("eb1_sb", [128, 32], F32))
    eb2_sb = es.enter_context(nc.sbuf_tensor("eb2_sb", [128, 8], F32))
    bout_s = es.enter_context(nc.sbuf_tensor("bout_s", [128, 8], F32))
    gam_s = es.enter_context(nc.sbuf_tensor("gam_s", [128, 8], F32))
    bet_s = es.enter_context(nc.sbuf_tensor("bet_s", [128, 8], F32))
    b1a_s = es.enter_context(nc.sbuf_tensor("b1a_s", [128, 32], F32))
    b2a_s = es.enter_context(nc.sbuf_tensor("b2a_s", [128, 8], F32))
    identb = es.enter_context(nc.sbuf_tensor("identb", [128, 128], BF16))
    tidx_a = es.enter_context(nc.sbuf_tensor("tidx_a", [128, NST], U32))
    tidx_f = es.enter_context(nc.sbuf_tensor("tidx_f", [128, 2 * NST], F32))
    yidx_a = es.enter_context(nc.sbuf_tensor("yidx_a", [128, NT * 2], U32))
    xgl = es.enter_context(nc.sbuf_tensor("xgl", [128, 2 * D], F32))
    xgb = es.enter_context(nc.sbuf_tensor("xgb", [128, 2 * D], BF16))
    disp_fm = es.enter_context(nc.sbuf_tensor("disp_fm", [128, 2 * NCH * 256], BF16))
    h_fm = es.enter_context(nc.sbuf_tensor("h_fm", [128, 32 * 256], BF16))
    y_fm = es.enter_context(nc.sbuf_tensor("y_fm", [128, 8 * 256], BF16))
    y_tm = es.enter_context(nc.sbuf_tensor("y_tm", [128, D], BF16))
    sums = es.enter_context(nc.sbuf_tensor("sums", [128, NT], F32))
    ssqs = es.enter_context(nc.sbuf_tensor("ssqs", [128, NT], F32))
    mu_a = es.enter_context(nc.sbuf_tensor("mu_a", [128, NT], F32))
    var_a = es.enter_context(nc.sbuf_tensor("var_a", [128, NT], F32))
    rstd_a = es.enter_context(nc.sbuf_tensor("rstd_a", [128, NT], F32))
    e0_a = es.enter_context(nc.sbuf_tensor("e0_a", [128, NT], F32))
    e1_a = es.enter_context(nc.sbuf_tensor("e1_a", [128, NT], F32))
    m1_a = es.enter_context(nc.sbuf_tensor("m1_a", [128, NT], F32))
    m2_a = es.enter_context(nc.sbuf_tensor("m2_a", [128, NT], F32))
    d0_a = es.enter_context(nc.sbuf_tensor("d0_a", [128, NT], F32))
    d1_a = es.enter_context(nc.sbuf_tensor("d1_a", [128, NT], F32))
    g0_a = es.enter_context(nc.sbuf_tensor("g0_a", [128, NT], F32))
    g1_a = es.enter_context(nc.sbuf_tensor("g1_a", [128, NT], F32))
    pos_a = es.enter_context(nc.sbuf_tensor("pos_a", [128, NT * 2], F32))
    keep_a = es.enter_context(nc.sbuf_tensor("keep_a", [128, NT * 2], F32))
    sidx_f = es.enter_context(nc.sbuf_tensor("sidx_f", [128, NT * 2], F32))
    sidx_a = es.enter_context(nc.sbuf_tensor("sidx_a", [128, NT * 2], U32))
    lg_all = es.enter_context(nc.sbuf_tensor("lg_all", [128, NT * 8], F32))
    oh_all = es.enter_context(nc.sbuf_tensor("oh_all", [128, NT * 16], F32))
    fns_a = es.enter_context(nc.sbuf_tensor("fns_a", [128, NT * 16], F32))
    cnt_sb = es.enter_context(nc.sbuf_tensor("cnt_sb", [16, 1], F32))
    cntg = es.enter_context(nc.sbuf_tensor("cntg", [128, 1], F32))
    tot_sb = es.enter_context(nc.sbuf_tensor("tot_sb", [16, NT], F32))
    totT_sb = es.enter_context(nc.sbuf_tensor("totT_sb", [NT, 16], F32))
    basef = es.enter_context(nc.sbuf_tensor("basef", [1, 16], F32))
    runp_sb = es.enter_context(nc.sbuf_tensor("runp_sb", [NT, 16], F32))
    myrec = es.enter_context(nc.sbuf_tensor("myrec", [128, NST * 4], F32))
    ps_tr = es.enter_context(nc.psum_tensor("ps_tr", [128, 512], F32))
    ps_tr2 = es.enter_context(nc.psum_tensor("ps_tr2", [128, 512], F32))
    ps_lg = es.enter_context(nc.psum_tensor("ps_lg", [128, 512], F32))
    ps_pos = es.enter_context(nc.psum_tensor("ps_pos", [128, 512], F32))
    ps_m0 = es.enter_context(nc.psum_tensor("ps_m0", [128, 512], F32))
    ps_m1 = es.enter_context(nc.psum_tensor("ps_m1", [128, 512], F32))
    ps_m2 = es.enter_context(nc.psum_tensor("ps_m2", [128, 512], F32))
    ps_m3 = es.enter_context(nc.psum_tensor("ps_m3", [128, 512], F32))
    s_l1 = [es.enter_context(nc.semaphore(f"s_l1{j}")) for j in range(2)]
    s_l2 = [es.enter_context(nc.semaphore(f"s_l2{j}")) for j in range(2)]
    s_const = es.enter_context(nc.semaphore("s_const"))
    s_sq = es.enter_context(nc.semaphore("s_sq"))
    s_red = es.enter_context(nc.semaphore("s_red"))
    s_bst = es.enter_context(nc.semaphore("s_bst"))
    s_rst = es.enter_context(nc.semaphore("s_rst"))
    s_xn = es.enter_context(nc.semaphore("s_xn"))
    s_tr = es.enter_context(nc.semaphore("s_tr"))
    s_evf = es.enter_context(nc.semaphore("s_evf"))
    s_ebf = es.enter_context(nc.semaphore("s_ebf"))
    s_lgm = es.enter_context(nc.semaphore("s_lgm"))
    s_lgv = es.enter_context(nc.semaphore("s_lgv"))
    s_t2 = es.enter_context(nc.semaphore("s_t2"))
    s_sig = es.enter_context(nc.semaphore("s_sig"))
    s_tot = es.enter_context(nc.semaphore("s_tot"))
    s_mid = es.enter_context(nc.semaphore("s_mid"))
    s_mg = es.enter_context(nc.semaphore("s_mg"))
    s_cc = es.enter_context(nc.semaphore("s_cc"))
    s_mt = es.enter_context(nc.semaphore("s_mt"))
    s_fp = es.enter_context(nc.semaphore("s_fp"))
    s_fv = es.enter_context(nc.semaphore("s_fv"))
    s_sc = es.enter_context(nc.semaphore("s_sc"))
    s_gr = es.enter_context(nc.semaphore("s_gr"))
    s_out = es.enter_context(nc.semaphore("s_out"))
    s_ew = es.enter_context(nc.semaphore("s_ew"))
    s_tix = es.enter_context(nc.semaphore("s_tix"))
    s_xg = [es.enter_context(nc.semaphore(f"s_xg{j}")) for j in range(2)]
    s_lnv = es.enter_context(nc.semaphore("s_lnv"))
    s_cst = es.enter_context(nc.semaphore("s_cst"))
    s_trB = es.enter_context(nc.semaphore("s_trB"))
    s_evB = es.enter_context(nc.semaphore("s_evB"))
    s_hm = es.enter_context(nc.semaphore("s_hm"))
    s_hev = es.enter_context(nc.semaphore("s_hev"))
    s_ya = es.enter_context(nc.semaphore("s_ya"))
    s_yev = es.enter_context(nc.semaphore("s_yev"))
    s_ytr = es.enter_context(nc.semaphore("s_ytr"))
    s_ytv = es.enter_context(nc.semaphore("s_ytv"))
    s_ym = [es.enter_context(nc.semaphore(f"s_ym{j}")) for j in range(2)]
    s_yg = [es.enter_context(nc.semaphore(f"s_yg{j}")) for j in range(2)]
    s_cmb = es.enter_context(nc.semaphore("s_cmb"))
    s_wk = es.enter_context(nc.semaphore("s_wk"))
    s_akm = es.enter_context(nc.semaphore("s_akm"))
    s_akv = es.enter_context(nc.semaphore("s_akv"))
    s_avm = es.enter_context(nc.semaphore("s_avm"))
    s_avv = es.enter_context(nc.semaphore("s_avv"))
    s_akp = es.enter_context(nc.semaphore("s_akp"))
    s_akx = es.enter_context(nc.semaphore("s_akx"))
    s_akf = es.enter_context(nc.semaphore("s_akf"))
    s_akk = es.enter_context(nc.semaphore("s_akk"))
    s_akvd = es.enter_context(nc.semaphore("s_akvd"))
    s_kva = es.enter_context(nc.semaphore("s_kva"))
    s_kvin = es.enter_context(nc.semaphore("s_kvin"))
    s_kvl = es.enter_context(nc.semaphore("s_kvl"))
    s_wq2 = es.enter_context(nc.semaphore("s_wq2"))
    s_kvb = es.enter_context(nc.semaphore("s_kvb"))
    s_aw = es.enter_context(nc.semaphore("s_aw"))
    s_o1 = [es.enter_context(nc.semaphore(f"s_o1{j}")) for j in range(2)]
    s_moA = es.enter_context(nc.semaphore("s_moA"))
    s_awq = [es.enter_context(nc.semaphore(f"s_awq{j}")) for j in range(2)]
    s_aw2 = [es.enter_context(nc.semaphore(f"s_aw2{j}")) for j in range(2)]
    s_l3 = s_l1; B_l3 = 256
    s_atr = s_tr; B_atr = 64
    s_aev = s_evf; B_aev = 64
    s_axn = s_xn; B_axn = 32
    s_aqm = s_lgm; B_aqm = 32
    s_aqv = s_lgv; B_aqv = 32
    s_asq = s_sq; B_asq = 32
    s_ars = s_rst; B_ars = 1
    s_ast = s_bst; B_ast = 1
    s_asm = s_mt; B_asm = 3
    s_arw = s_fp; B_arw = 1
    s_a4t = s_tot; B_a4t = 1
    s_aqf = s_akp; B_aqf = 512
    s_aqx = s_akx; B_aqx = 512
    s_aqe = s_akf; B_aqe = 512
    s_azm = s_akm; B_azm = 64
    s_azv = s_akv; B_azv = 64
    s_azb = s_avm; B_azb = 64
    s_aqz = s_avv; B_aqz = 64
    s_aam = s_akk; B_aam = 1024
    s_aov = s_akvd; B_aov = 1
    s_axm = s_hm; B_axm = 1024
    s_axv = s_fv; B_axv = 1
    s_ax2 = s_lnv; B_ax2 = 64
    s_a2b = s_cst; B_a2b = 64
    s_ahm = s_ya; B_ahm = 64
    s_ahe = s_hev; B_ahe = 1024
    s_aw2c = s_trB; B_aw2c = 64
    s_axe = s_ytv; B_axe = 64
    s_ax4 = s_yev; B_ax4 = 64
    s_a4v = s_mid; B_a4v = 4
    s_awo = s_const_l = [s_const, s_const]; B_awo = 208
    s_aw1 = [s_kvin, s_kvin]; B_aw1 = 16
    block = es.enter_context(nc.Block())
    KSB0 = 16384
    KFB0 = KSB0 + 4096
    V650 = KFB0 + 512
    EXP0 = V650 + 4160
    PRJ0 = EXP0 + 512
    H2A0, W1B0, W2B0, WQB0, PRJ2, WOB0, QX0 = 0, 16384, 18432, 20480, 22528, 26624, 28672
    X1C, X2C, X2BC, AOC, QSC, QFC, OUTC = 0, 8192, 16384, 20480, 24576, 25600, 26624
    with es:
        NCONST = 18
        lv = {"A": 0, "B": 1, "C1": 2, "C2": 3, "C": 9}[stage]
        CC_YAG = 4 if lv >= 2 else 3
        stt = m1_a

        @block.sync
        def _(sync):
            for dst, src in [
                (ident[:], ident_i[:, :]), (lstrict[:], lstrict_i[:, :]),
                (iota8[:], iota8_i[:, :]), (ones1[:], ones1_i[:, :]),
                (onesc[:], onesc_i[:, :]), (basemat[:], basemat_i[:, :]),
                (jall[:], jall_i[:, :]), (gidxc[:], gidx_i[:, :]),
                (epsc[:], eps_i[:, :]), (wg_sb[:], wg[:, :]),
                (eb1_sb[:], eb1_i[:, :]), (eb2_sb[:], eb2_i[:, :]),
                (identb[:], identb_i[:, :]), (bout_s[:], bout_c[:, :]),
                (gam_s[:], gam_c[:, :]), (bet_s[:], bet_c[:, :]),
                (b1a_s[:], b1a_c[:, :]), (b2a_s[:], b2a_c[:, :]),
            ]:
                sync.dma_start(out=dst, in_=src).then_inc(s_const, 16)
            for i in range(NT):
                if i >= 2:
                    sync.wait_ge(s_sq, i - 1)
                    sync.wait_ge(s_red, i - 1)
                sync.dma_start(
                    out=xt[:, (i % 2) * D:(i % 2 + 1) * D],
                    in_=xm[i * 128:(i + 1) * 128, :],
                ).then_inc(s_l1[i % 2], 16)
            for i in range(NT):
                if i == 0:
                    sync.wait_ge(s_red, NT)
                    sync.wait_ge(s_sq, NT)
                if i >= 2:
                    sync.wait_ge(s_xn, i - 1)
                sync.dma_start(
                    out=xt[:, (i % 2) * D:(i % 2 + 1) * D],
                    in_=xm[i * 128:(i + 1) * 128, :],
                ).then_inc(s_l2[i % 2], 16)
            if lv >= 2:
                sync.wait_ge(s_akvd, 1)
                sync.dma_start(out=kv_in[:, :], in_=h_fm[:, 0:4160].bitcast(F32)
                               ).then_inc(s_kvin, 16)
            for st in range(NST):
                sync.wait_ge(s_ytv, st + 1)
                sync.dma_start(out=y_mine[st * 128:(st + 1) * 128, :],
                               in_=y_tm[:, 0:D]
                               ).then_inc(s_ym[st % 2], 16)

            for tg in range(8 if lv >= 9 else 0):
                for tt in range(4):
                    gt = tg * 4 + tt
                    if gt >= 2:
                        sync.wait_ge(s_axn, B_axn + gt - 1)
                    if tg == 0 and tt == 0:
                        sync.wait_ge(s_ya, 64)
                    sync.dma_start(
                        out=xt[:, (gt % 2) * D:(gt % 2 + 1) * D],
                        in_=xm[gt * 128:(gt + 1) * 128, :],
                    ).then_inc(s_l3[gt % 2], 16)
                for qc in range(8):
                    u = tg * 8 + qc
                    if u >= 2:
                        sync.wait_ge(s_aqm, B_aqm + u - 1)
                    sync.dma_start(
                        out=ew2_sb[:, WQB0 + (u % 2) * 1024:WQB0 + (u % 2 + 1) * 1024],
                        in_=bass.AP(wq_bf.ap().tensor, wq_bf.ap().offset + qc * 128,
                                    [[8192, 128], [1024, 8], [1, 128]]),
                    ).then_inc(s_awq[u % 2], 16)
                for dc in range(8):
                    u = tg * 8 + dc
                    if u >= 2:
                        sync.wait_ge(s_axm, B_axm + u - 1)
                    sync.dma_start(
                        out=ew2_sb[:, WOB0 + (u % 2) * 1024:WOB0 + (u % 2 + 1) * 1024],
                        in_=wout_bf[:, dc * 1024:(dc + 1) * 1024],
                    ).then_inc(s_awo[0], 16)
                for hch in range(32):
                    u = tg * 32 + hch
                    if u >= 2:
                        sync.wait_ge(s_ahm, B_ahm + u - 1)
                    sync.dma_start(
                        out=ew2_sb[:, W1B0 + (u % 2) * 1024:W1B0 + (u % 2 + 1) * 1024],
                        in_=w1a_bf[:, hch * 1024:(hch + 1) * 1024],
                    ).then_inc(s_aw1[0], 16)
                for pss in range(2):
                    for hch in range(32):
                        u = tg * 64 + pss * 32 + hch
                        if u >= 2:
                            sync.wait_ge(s_aw2c, B_aw2c + u - 1)
                        sync.dma_start(
                            out=ew2_sb[:, W2B0 + (u % 2) * 1024:W2B0 + (u % 2 + 1) * 1024],
                            in_=w2a_bf[:, hch * 1024:(hch + 1) * 1024],
                        ).then_inc(s_aw2[u % 2], 16)
                for tt in range(4):
                    gt = tg * 4 + tt
                    sync.wait_ge(s_a4v, B_a4v + gt + 1)
                    sync.dma_start(
                        out=out_d[gt * 128:(gt + 1) * 128, :],
                        in_=ew1_sb[:, OUTC + (gt % 2) * 2048:OUTC + (gt % 2) * 2048 + 2048]
                        .bitcast(F32),
                    ).then_inc(s_o1[gt % 2], 16)

            sync.wait_ge(s_gr, 16 * NST)
            sync.wait_ge(s_fv, 1)
            sync.dma_start(out=dbg_rt[:, 0 * NT:1 * NT], in_=e0_a[:]).then_inc(s_out, 16)
            sync.dma_start(out=dbg_rt[:, 1 * NT:2 * NT], in_=e1_a[:]).then_inc(s_out, 16)
            sync.dma_start(out=dbg_rt[:, 2 * NT:3 * NT], in_=g0_a[:]).then_inc(s_out, 16)
            sync.dma_start(out=dbg_rt[:, 3 * NT:4 * NT], in_=g1_a[:]).then_inc(s_out, 16)
            sync.dma_start(out=dbg_rt[:, 4 * NT:5 * NT], in_=mu_a[:]).then_inc(s_out, 16)
            sync.dma_start(out=dbg_rt[:, 5 * NT:6 * NT], in_=rstd_a[:]).then_inc(s_out, 16)
            sync.dma_start(out=dbg_rt[:, 6 * NT:8 * NT], in_=pos_a[:]).then_inc(s_out, 16)
            sync.dma_start(out=dbg_rec[:, :], in_=myrec[:]).then_inc(s_out, 16)
            sync.wait_ge(s_out, 16 * 8)
            if lv >= 9:
                sync.wait_ge(s_moA, 16 * NT)

        @block.scalar
        def _(scalar):
            scalar.wait_ge(s_const, 16 * NCONST)
            for i in range(NT):
                p2 = i % 2
                xs = xt[:, (i % 2) * D:(i % 2 + 1) * D]
                scalar.wait_ge(s_l1[i % 2], 16 * (i // 2 + 1))
                if i >= 2:
                    scalar.wait_ge(s_red, i - 1)
                scalar.activation(xn_tm[:, p2 * D:(p2 + 1) * D], xs,
                                  AF.Square).then_inc(s_sq, 1)
            scalar.wait_ge(s_bst, 1)
            scalar.activation(var_a[:], var_a[:], AF.Ln, bias=epsc[:, 0:1])
            scalar.drain()
            scalar.activation(rstd_a[:], var_a[:], AF.Exp, scale=-0.5)
            scalar.drain().then_inc(s_rst, 1)
            for i in range(NT):
                p2 = i % 2
                scalar.wait_ge(s_evf, 2 * i + 2)
                for ch in range(NCH):
                    ins = scalar.copy(
                        xn_fm[:, ch * TPC + i * 128: ch * TPC + (i + 1) * 128],
                        xn_f32[:, p2 * D + ch * 128: p2 * D + (ch + 1) * 128],
                    )
                ins.then_inc(s_ebf, 1)
            scalar.wait_ge(s_t2, 1)
            scalar.activation(g0_a[:], d0_a[:], AF.Sigmoid)
            scalar.activation(g1_a[:], d1_a[:], AF.Sigmoid)
            scalar.drain().then_inc(s_sig, 1)

            for tg in range(8 if lv >= 2 else 0):
                for tt in range(4):
                    gtt = tg * 4 + tt
                    for h in range(16):
                        kp = gtt * 16 + h
                        scalar.wait_ge(s_akp, kp + 1)
                        if kp >= 2:
                            scalar.wait_ge(s_akf, kp - 1)
                        scalar.activation(
                            ew2_sb[:, EXP0 + (kp % 2) * 256:EXP0 + (kp % 2 + 1) * 256],
                            ps_lg[:, (h % 2) * 256:(h % 2) * 256 + 256], AF.Exp)
                        scalar.drain().then_inc(s_akx, 1)

            for st in range(NST):
                scalar.wait_ge(s_lnv, st + 1)
                if st >= 2:
                    scalar.wait_ge(s_trB, st - 1)
                scalar.activation(xgb[:, (st % 2) * D:(st % 2 + 1) * D],
                                  xgl[:, (st % 2) * D:(st % 2 + 1) * D],
                                  AF.Copy)
                scalar.drain().then_inc(s_cst, 1)
                if st % 2 == 1:
                    g = st // 2
                    if g == 0 and lv >= 2:
                        scalar.wait_ge(s_kvin, 16)
                    for hch in range(32):
                        scalar.wait_ge(s_hm, g * 32 + hch + 1)
                        src = ps_lg if hch % 2 == 0 else ps_pos
                        scalar.activation(h_fm[:, hch * 256:(hch + 1) * 256],
                                          src[:, 0:256],
                                          AF.Gelu, bias=eb1_sb[:, hch:hch + 1])
                        scalar.drain().then_inc(s_hev, 1)


            if lv < 9:
                return
            scalar.wait_ge(s_ya, 64)
            XSs = ew1_sb[:, 0:8192].bitcast(F32)
            SQs = ew1_sb[:, 8192:16384].bitcast(F32)
            for tg in range(8):
                for qc in range(8):
                    for hq in range(2):
                        hh = tg * 16 + qc * 2 + hq
                        for fh in range(2):
                            c_qf = hh * 2 + fh
                            scalar.wait_ge(s_aqf, B_aqf + c_qf + 1)
                            if c_qf >= 2:
                                scalar.wait_ge(s_aqe, B_aqe + c_qf - 1)
                            scalar.activation(
                                ew2_sb[:, QX0 + (c_qf % 2) * 512:QX0 + (c_qf % 2 + 1) * 512],
                                ps_m3[:, 0:512], AF.Exp)
                            scalar.drain().then_inc(s_aqx, 1)
                for dc in range(8):
                    scalar.wait_ge(s_axv, B_axv + tg * 8 + dc + 1)
                    scalar.activation(
                        bass.AP(SQs.tensor, SQs.offset + dc * 512,
                                [[SQs.ap[0][0], 128], [1, 512]]),
                        bass.AP(XSs.tensor, XSs.offset + dc * 512,
                                [[XSs.ap[0][0], 128], [1, 512]]),
                        AF.Square)
                    scalar.drain().then_inc(s_asq, 1)
                scalar.wait_ge(s_ast, B_ast + (tg * 2) * 2 + 1)
                scalar.activation(stt[:, 8:12], stt[:, 4:8], AF.Ln, bias=epsc[:, 0:1])
                scalar.drain()
                scalar.activation(stt[:, 8:12], stt[:, 8:12], AF.Exp, scale=-0.5)
                scalar.drain().then_inc(s_ars, 1)
                for dc in range(8):
                    scalar.wait_ge(s_ax2, B_ax2 + tg + 1)
                    scalar.copy(
                        ew1_sb[:, X2BC + dc * 512:X2BC + (dc + 1) * 512],
                        bass.AP(SQs.tensor, SQs.offset + dc * 512,
                                [[SQs.ap[0][0], 128], [1, 512]]))
                    scalar.drain().then_inc(s_a2b, 1)
                for hch in range(32):
                    uu = tg * 32 + hch
                    scalar.wait_ge(s_ahm, B_ahm + uu + 1)
                    srch = ps_lg if hch % 2 == 0 else ps_pos
                    scalar.activation(ew2_sb[:, H2A0 + hch * 512:H2A0 + (hch + 1) * 512],
                                      srch[:, 0:512], AF.Gelu,
                                      bias=b1a_s[:, hch:hch + 1])
                    scalar.drain().then_inc(s_ahe, 1)
                for dc in range(8):
                    scalar.wait_ge(s_axe, B_axe + tg * 2 + 2)
                    scalar.activation(
                        bass.AP(SQs.tensor, SQs.offset + dc * 512,
                                [[SQs.ap[0][0], 128], [1, 512]]),
                        bass.AP(XSs.tensor, XSs.offset + dc * 512,
                                [[XSs.ap[0][0], 128], [1, 512]]),
                        AF.Square)
                    scalar.drain().then_inc(s_asq, 1)
                scalar.wait_ge(s_ast, B_ast + (tg * 2 + 1) * 2 + 1)
                scalar.activation(stt[:, 8:12], stt[:, 4:8], AF.Ln, bias=epsc[:, 0:1])
                scalar.drain()
                scalar.activation(stt[:, 8:12], stt[:, 8:12], AF.Exp, scale=-0.5)
                scalar.drain().then_inc(s_ars, 1)

        @block.vector
        def _(vector):
            vector.wait_ge(s_const, 16 * NCONST)
            for i in range(NT):
                p2 = i % 2
                xs = xt[:, (i % 2) * D:(i % 2 + 1) * D]
                vector.wait_ge(s_l1[i % 2], 16 * (i // 2 + 1))
                vector.tensor_reduce(sums[:, i:i + 1], xs, AX.X, ALU.add)
                vector.wait_ge(s_sq, i + 1)
                vector.tensor_reduce(ssqs[:, i:i + 1],
                                     xn_tm[:, p2 * D:(p2 + 1) * D], AX.X,
                                     ALU.add).then_inc(s_red, 1)
            vector.drain()
            vector.tensor_scalar(mu_a[:], sums[:], 1.0 / D, None, ALU.mult)
            vector.tensor_scalar(var_a[:], ssqs[:], 1.0 / D, None, ALU.mult)
            vector.drain()
            vector.tensor_tensor(fns_a[:, 0:NT], mu_a[:], mu_a[:], ALU.mult)
            vector.drain()
            vector.tensor_tensor(var_a[:], var_a[:], fns_a[:, 0:NT], ALU.subtract)
            vector.drain().then_inc(s_bst, 1)
            vector.wait_ge(s_rst, 1)
            for i in range(NT):
                p2 = i % 2
                xs = xt[:, (i % 2) * D:(i % 2 + 1) * D]
                vector.wait_ge(s_l2[i % 2], 16 * (i // 2 + 1))
                xnd = xn_tm[:, p2 * D:(p2 + 1) * D]
                if i >= 2:
                    vector.wait_ge(s_tr, 2 * (i - 1))
                vector.tensor_scalar(xnd, xs, mu_a[:, i:i + 1], rstd_a[:, i:i + 1],
                                     ALU.subtract, ALU.mult)
                vector.drain().then_inc(s_xn, 1)
                if i >= 2:
                    vector.wait_ge(s_lgm, i - 1)
                    vector.wait_ge(s_ebf, i - 1)
                for h2 in range(2):
                    vector.wait_ge(s_tr, 2 * i + h2 + 1)
                    src = ps_tr if h2 == 0 else ps_tr2
                    vector.tensor_copy(
                        xn_f32[:, p2 * D + h2 * 512: p2 * D + (h2 + 1) * 512],
                        src[:])
                    vector.drain().then_inc(s_evf, 1)
                vector.wait_ge(s_lgm, i + 1)
                vector.tensor_copy(lg_all[:, i * 8:(i + 1) * 8],
                                   ps_lg[:, (i % 2) * 16:(i % 2) * 16 + 8])
                vector.drain().then_inc(s_lgv, 1)

            LG = view3(lg_all, 128, NT, 8)
            io_ap = iota8.ap()
            IOB = bass.AP(io_ap.tensor, io_ap.offset,
                          [list(io_ap.ap[0]), [0, NT], [1, 8]])
            W1 = view3(fns_a, 128, NT, 8)
            W2 = view3(fns_a, 128, NT, 8, offset=NT * 8)
            OH0 = view3(oh_all, 128, NT, 8, offset=0, gstride=16)
            OH1 = view3(oh_all, 128, NT, 8, offset=8, gstride=16)
            vector.tensor_reduce(m1_a[:], LG, AX.X, ALU.max)
            vector.drain()
            vector.tensor_tensor(W1, LG, bcast3(m1_a, 128, NT, 8), ALU.is_equal)
            vector.tensor_tensor(W2, W1, IOB, ALU.mult)
            vector.tensor_scalar(W1, W1, -OOB, OOB, ALU.mult, ALU.add)
            vector.tensor_tensor(W2, W2, W1, ALU.add)
            vector.tensor_reduce(e0_a[:], W2, AX.X, ALU.min)
            vector.drain()
            vector.tensor_tensor(OH0, IOB, bcast3(e0_a, 128, NT, 8), ALU.is_equal)
            vector.drain()
            vector.tensor_tensor(W1, OH0, bcast3(m1_a, 128, NT, 8), ALU.bypass) if False else None
            vector.tensor_scalar(W1, OH0, OOB, None, ALU.mult)
            vector.tensor_tensor(W2, LG, W1, ALU.subtract)
            vector.tensor_reduce(m2_a[:], W2, AX.X, ALU.max)
            vector.drain()
            vector.tensor_tensor(W1, W2, bcast3(m2_a, 128, NT, 8), ALU.is_equal)
            vector.tensor_tensor(W2, W1, IOB, ALU.mult)
            vector.tensor_scalar(W1, W1, -OOB, OOB, ALU.mult, ALU.add)
            vector.tensor_tensor(W2, W2, W1, ALU.add)
            vector.tensor_reduce(e1_a[:], W2, AX.X, ALU.min)
            vector.drain()
            vector.tensor_tensor(OH1, IOB, bcast3(e1_a, 128, NT, 8), ALU.is_equal)
            vector.tensor_tensor(d0_a[:], m1_a[:], m2_a[:], ALU.subtract)
            vector.tensor_tensor(d1_a[:], m2_a[:], m1_a[:], ALU.subtract)
            vector.drain().then_inc(s_t2, 1)

            vector.wait_ge(s_tot, 1)
            vector.tensor_copy(tot_sb[:], ps_lg[0:16, 32:64])
            vector.drain()
            vector.tensor_reduce(cnt_sb[:], tot_sb[:], AX.X, ALU.add)
            vector.drain().then_inc(s_mid, 1)
            vector.wait_ge(s_mt, 1)
            vector.tensor_copy(basef[:], ps_lg[0:1, 64:80])
            vector.drain().then_inc(s_mid, 1)
            vector.wait_ge(s_mt, 2)
            vector.tensor_copy(totT_sb[:], ps_pos[0:NT, 96:112])
            vector.drain().then_inc(s_mid, 1)
            vector.wait_ge(s_mt, 3)
            vector.tensor_copy(runp_sb[:], ps_pos[0:NT, 64:80])
            vector.drain().then_inc(s_mid, 1)

            vector.wait_ge(s_fp, 1)
            psp_ap = ps_pos.ap()
            PSP = bass.AP(psp_ap.tensor, psp_ap.offset,
                          [list(psp_ap.ap[0]), [8, NT * 2], [1, 8]])
            FN = view3(fns_a, 128, NT * 2, 8)
            OHB = view3(oh_all, 128, NT * 2, 8)
            vector.tensor_tensor(FN, OHB, PSP, ALU.mult)
            vector.tensor_reduce(pos_a[:], FN, AX.X, ALU.add)
            vector.drain()
            vector.tensor_scalar(keep_a[:], pos_a[:], float(CAP), None, ALU.is_lt)
            vector.drain()
            G0V = view3(g0_a, 128, NT, 1)
            G1V = view3(g1_a, 128, NT, 1)
            K0 = view3(keep_a, 128, NT, 1, offset=0, gstride=2)
            K1 = view3(keep_a, 128, NT, 1, offset=1, gstride=2)
            vector.wait_ge(s_sig, 1)
            vector.tensor_tensor(G0V, G0V, K0, ALU.mult)
            vector.tensor_tensor(G1V, G1V, K1, ALU.mult)
            S0 = view3(sidx_f, 128, NT, 1, offset=0, gstride=2)
            S1 = view3(sidx_f, 128, NT, 1, offset=1, gstride=2)
            vector.tensor_scalar(S0, view3(e0_a, 128, NT, 1), float(CAP), None, ALU.mult)
            vector.tensor_scalar(S1, view3(e1_a, 128, NT, 1), float(CAP), None, ALU.mult)
            vector.drain()
            vector.tensor_tensor(sidx_f[:], sidx_f[:], pos_a[:], ALU.add)
            vector.tensor_scalar(fns_a[:, 0:NT * 2], keep_a[:], -OOB, OOB,
                                 ALU.mult, ALU.add)
            vector.drain()
            vector.tensor_tensor(sidx_f[:], sidx_f[:], fns_a[:, 0:NT * 2], ALU.add)
            vector.drain()
            vector.tensor_copy(sidx_a[:], sidx_f[:])
            rj_ap = lg_all.ap()
            RJ = bass.AP(rj_ap.tensor, rj_ap.offset,
                         [list(rj_ap.ap[0]), [4, NT * 2], [1, 1]])
            vector.tensor_copy(RJ, view3(jall, 128, NT * 2, 1))
            RM = bass.AP(rj_ap.tensor, rj_ap.offset + 1,
                         [list(rj_ap.ap[0]), [8, NT], [4, 2], [1, 1]])
            vector.tensor_copy(RM, bcast_mid(mu_a, 128, NT, 2, 1))
            RS = bass.AP(rj_ap.tensor, rj_ap.offset + 2,
                         [list(rj_ap.ap[0]), [8, NT], [4, 2], [1, 1]])
            vector.tensor_copy(RS, bcast_mid(rstd_a, 128, NT, 2, 1))
            RZ = bass.AP(rj_ap.tensor, rj_ap.offset + 3,
                         [list(rj_ap.ap[0]), [4, NT * 2], [1, 1]])
            vector.memset(RZ, 0.0)
            vector.drain()
            vector.tensor_scalar(tidx_f[:, 0:NT * 2], keep_a[:], -OOB, OOB,
                                 ALU.mult, ALU.add)
            vector.drain()
            vector.tensor_tensor(tidx_f[:, 0:NT * 2], sidx_f[:],
                                 tidx_f[:, 0:NT * 2], ALU.subtract)
            vector.drain()
            vector.tensor_tensor(tidx_f[:, 0:NT * 2], tidx_f[:, 0:NT * 2],
                                 keep_a[:], ALU.mult)
            vector.drain()
            vector.tensor_copy(yidx_a[:], tidx_f[:, 0:NT * 2])
            vector.drain().then_inc(s_fv, 1)

            vector.wait_ge(s_gr, 16 * NST)
            MRJ = bass.AP(myrec.ap().tensor, myrec.ap().offset,
                          [list(myrec.ap().ap[0]), [4, NST], [1, 1]])
            MRR = bass.AP(myrec.ap().tensor, myrec.ap().offset + 2,
                          [list(myrec.ap().ap[0]), [4, NST], [1, 1]])
            TF0 = view3(tidx_f, 128, NST, 1)
            TF1 = view3(tidx_f, 128, NST, 1, offset=NST)
            vector.tensor_scalar(TF1, MRR, 0.0, None, ALU.is_gt)
            vector.drain()
            vector.tensor_scalar(tidx_f[:, NST:2 * NST], tidx_f[:, NST:2 * NST],
                                 -float(1 << 27), float(1 << 27), ALU.mult, ALU.add)
            vector.drain()
            vector.tensor_tensor(TF0, MRJ, TF1, ALU.add)
            vector.drain()
            vector.tensor_copy(tidx_a[:], tidx_f[:, 0:NST])
            vector.drain()
            vector.tensor_scalar(tidx_a[:], tidx_a[:], 1, None,
                                 ALU.arith_shift_right)
            vector.drain().then_inc(s_tix, 1)

            KVB = [ps_m0, ps_m1, ps_m2, ps_m3, ps_pos]
            KVACC = h_fm[:, 0:4160].bitcast(F32)
            if lv >= 2:
                vector.memset(ew2_sb[:, V650:V650 + 4160], 1.0)
                vector.memset(KVACC[:, 0:2080], 0.0)
                vector.drain()
            for tg in range(8 if lv >= 2 else 0):
                for oc in range(8):
                    ktg = tg * 8 + oc
                    vector.wait_ge(s_akm, ktg + 1)
                    if tg >= 1:
                        vector.wait_ge(s_akp, tg * 64)
                    vector.tensor_copy(
                        ew2_sb[:, KSB0 + oc * 512:KSB0 + (oc + 1) * 512],
                        ps_tr[:, 0:512])
                    vector.drain().then_inc(s_akv, 1)
                for tt in range(4):
                    gtt = tg * 4 + tt
                    for vh in range(2):
                        idx_v = gtt * 2 + vh
                        vector.wait_ge(s_avm, idx_v + 1)
                        if gtt >= 4:
                            vector.wait_ge(s_akk, 32 * (gtt - 3))
                        dst = bass.AP(ew2_sb.ap().tensor,
                                      ew2_sb.ap().offset + V650 + (gtt % 4) * 1040 + vh * 8 * 65,
                                      [[ew2_sb.ap().ap[0][0], 128], [65, 8], [1, 64]])
                        src = bass.AP(ps_tr2.ap().tensor, ps_tr2.ap().offset,
                                      [[512, 128], [64, 8], [1, 64]])
                        vector.tensor_copy(dst, src)
                        vector.drain().then_inc(s_avv, 1)
                for tt in range(4):
                    gtt = tg * 4 + tt
                    for h in range(16):
                        kp = gtt * 16 + h
                        vector.wait_ge(s_akx, kp + 1)
                        if kp >= 2:
                            vector.wait_ge(s_akk, 2 * (kp - 1))
                        vector.tensor_scalar(fns_a[:, 0:256],
                                             ps_lg[:, (h % 2) * 256:(h % 2) * 256 + 256],
                                             0.0, None, ALU.max)
                        vector.tensor_scalar(fns_a[:, 256:512],
                                             ew2_sb[:, EXP0 + (kp % 2) * 256:
                                                    EXP0 + (kp % 2 + 1) * 256],
                                             1.0, None, ALU.min)
                        vector.drain()
                        vector.tensor_tensor(
                            ew2_sb[:, KFB0 + (kp % 2) * 256:KFB0 + (kp % 2 + 1) * 256],
                            fns_a[:, 256:512], fns_a[:, 0:256], ALU.add)
                        vector.drain().then_inc(s_akf, 1)
                    vector.wait_ge(s_akk, 2 * 16 * (gtt + 1))
                    for b in range(4):
                        vector.tensor_tensor(KVACC[:, b * 512:(b + 1) * 512],
                                             KVACC[:, b * 512:(b + 1) * 512],
                                             KVB[b][:, 0:512], ALU.add)
                    vector.tensor_tensor(KVACC[:, 2048:2080], KVACC[:, 2048:2080],
                                         ps_pos[:, 0:32], ALU.add)
                    vector.drain().then_inc(s_kva, 1)
            if lv >= 2:
                vector.wait_ge(s_kva, NT)
                vector.drain().then_inc(s_akvd, 1)

            for st in range(NST):
                g = st // 2
                vector.wait_ge(s_xg[st % 2], 16 * (st // 2 + 1))
                mu_c = myrec[:, st * 4 + 1:st * 4 + 2]
                rs_c = myrec[:, st * 4 + 2:st * 4 + 3]
                vector.tensor_scalar(xgl[:, (st % 2) * D:(st % 2 + 1) * D],
                                     xgl[:, (st % 2) * D:(st % 2 + 1) * D],
                                     mu_c, rs_c, ALU.subtract, ALU.mult)
                vector.drain().then_inc(s_lnv, 1)
                vector.wait_ge(s_trB, st + 1)
                if g >= 2:
                    vector.wait_ge(s_hm, (g - 1) * 32)
                pb = ps_tr[:].bitcast(BF16)
                vector.tensor_copy(
                    bass.AP(disp_fm.ap().tensor,
                            disp_fm.ap().offset + (g % 2) * 2048 + (st % 2) * 128,
                            [[disp_fm.ap().ap[0][0], 128], [256, NCH], [1, 128]]),
                    bass.AP(pb.tensor, pb.offset,
                            [[pb.ap[0][0], 128], [128, NCH], [1, 128]]))
                vector.drain().then_inc(s_evB, 1)
                if st % 2 == 1:
                    g2 = st // 2
                    for pss in range(2):
                        vector.wait_ge(s_ya, g2 * 2 + pss + 1)
                        if g2 >= 1 and pss == 0:
                            vector.wait_ge(s_ytr, 2 * g2)
                        for dq in range(4):
                            dch = pss * 4 + dq
                            src = [ps_m0, ps_m1, ps_m2, ps_m3][dq]
                            vector.tensor_scalar(y_fm[:, dch * 256:(dch + 1) * 256],
                                                 src[:, 0:256],
                                                 eb2_sb[:, dch:dch + 1], None, ALU.add)
                        vector.drain().then_inc(s_yev, 1)
                    for q in range(2):
                        st2 = g2 * 2 + q
                        vector.wait_ge(s_ytr, st2 + 1)
                        if st2 >= 1:
                            vector.wait_ge(s_ym[(st2 - 1) % 2], 16 * ((st2 - 1) // 2 + 1))
                        pb2 = ps_tr2[:].bitcast(BF16)
                        vector.tensor_copy(
                            y_tm[:, 0:D],
                            bass.AP(pb2.tensor, pb2.offset,
                                    [[pb2.ap[0][0], 128], [128, NCH], [1, 128]]))
                        vector.drain().then_inc(s_ytv, 1)

            if lv >= 9:
                vector.wait_ge(s_kvl, 16)
                vector.tensor_copy(h_fm[:, 4160:6240], h_fm[:, 0:4160].bitcast(F32))
                vector.drain().then_inc(s_kvb, 1)
                vector.wait_ge(s_hm, 32 * 32)
                XS = ew1_sb[:, 0:8192].bitcast(F32)
                SQ = ew1_sb[:, 8192:16384].bitcast(F32)
                MUB = ew1_sb[:, AOC:AOC + 1024].bitcast(F32)
                RSB = ew1_sb[:, AOC + 1024:AOC + 2048].bitcast(F32)
                SCR2 = ew1_sb[:, 30720:31744].bitcast(F32)
                for tg in range(8):
                    tgp = tg % 2
                    for tt in range(4):
                        gt = tg * 4 + tt
                        vector.wait_ge(s_l3[gt % 2], B_l3 + 16 * (gt // 2 + 1))
                        xres = (xgl[:, tt * D:(tt + 1) * D] if tt < 2
                                else xn_tm[:, (tt - 2) * D:(tt - 1) * D])
                        xsl = xt[:, (gt % 2) * D:(gt % 2 + 1) * D]
                        vector.tensor_copy(xres, xsl)
                        vector.tensor_scalar(xn_f32[:, (gt % 2) * D:(gt % 2 + 1) * D],
                                             xsl, mu_a[:, gt:gt + 1], rstd_a[:, gt:gt + 1],
                                             ALU.subtract, ALU.mult)
                        vector.drain().then_inc(s_axn, 1)
                        for h2 in range(2):
                            vector.wait_ge(s_atr, B_atr + 2 * gt + h2 + 1)
                            if tg >= 1 and tt == 0 and h2 == 0:
                                vector.wait_ge(s_axv, B_axv + 8 * tg)
                            src = ps_tr if h2 == 0 else ps_tr2
                            dstv = bass.AP(disp_fm.ap().tensor,
                                           disp_fm.ap().offset + h2 * 4 * 512 + tt * 128,
                                           [[disp_fm.ap().ap[0][0], 128], [512, 4], [1, 128]])
                            srcv = bass.AP(src.ap().tensor, src.ap().offset,
                                           [[512, 128], [128, 4], [1, 128]])
                            vector.tensor_copy(dstv, srcv)
                            vector.drain().then_inc(s_aev, 1)
                    for qc in range(8):
                        u = tg * 8 + qc
                        vector.wait_ge(s_aqm, B_aqm + u + 1)
                        vector.tensor_copy(ew1_sb[:, QSC + (u % 2) * 512:QSC + (u % 2 + 1) * 512],
                                           ps_m2[:, 0:512])
                        vector.drain().then_inc(s_aqv, 1)
                        for hq in range(2):
                            h = qc * 2 + hq
                            hh = tg * 16 + h
                            for fh in range(2):
                                c_qf = hh * 2 + fh
                                vector.wait_ge(s_aqx, B_aqx + c_qf + 1)
                                vector.tensor_scalar(fns_a[:, 0:512], ps_m3[:, 0:512],
                                                     0.0, None, ALU.max)
                                vector.tensor_scalar(SCR2[:, 0:512],
                                                     ew2_sb[:, QX0 + (c_qf % 2) * 512:
                                                            QX0 + (c_qf % 2 + 1) * 512],
                                                     1.0, None, ALU.min)
                                vector.drain()
                                vector.tensor_tensor(
                                    ew1_sb[:, QFC + fh * 512:QFC + (fh + 1) * 512],
                                    SCR2[:, 0:512], fns_a[:, 0:512], ALU.add)
                                vector.drain().then_inc(s_aqe, 1)
                            vector.wait_ge(s_azm, B_azm + hh + 1)
                            zrow = y_tm[:].bitcast(F32)[0:1, 0:512]
                            vector.tensor_scalar(zrow, ps_lg[0:1, 0:512], EPS_ATTN,
                                                 None, ALU.add)
                            vector.drain()
                            vector.reciprocal(zrow, zrow)
                            vector.drain().then_inc(s_azv, 1)
                            vector.wait_ge(s_azb, B_azb + hh + 1)
                            for fh in range(2):
                                vector.tensor_tensor(
                                    ew1_sb[:, QFC + fh * 512:QFC + (fh + 1) * 512],
                                    ew1_sb[:, QFC + fh * 512:QFC + (fh + 1) * 512],
                                    ps_lg[:, 0:512], ALU.mult)
                            vector.drain().then_inc(s_aqz, 1)
                        vector.wait_ge(s_aam, B_aam + tg * 16 + qc * 2 + 2)
                        bank = ps_m0 if qc % 2 == 0 else ps_m1
                        vector.tensor_copy(ew1_sb[:, AOC + qc * 512:AOC + (qc + 1) * 512],
                                           bank[:, 0:512])
                        vector.drain().then_inc(s_aov, 1)
                    for dc in range(8):
                        u = tg * 8 + dc
                        vector.wait_ge(s_axm, B_axm + u + 1)
                        bank = ps_m2 if dc % 2 == 0 else ps_m3
                        vector.scalar_tensor_tensor(
                            bass.AP(XS.tensor, XS.offset + dc * 512,
                                    [[XS.ap[0][0], 128], [1, 512]]),
                            bank[:, 0:512], bout_s[:, dc:dc + 1],
                            disp_fm[:, dc * 512:(dc + 1) * 512],
                            ALU.add, ALU.add)
                        vector.drain().then_inc(s_axv, 1)
                    for r in range(2):
                        lr = tg * 2 + r
                        vector.wait_ge(s_asm, B_asm + lr + 1)
                        vector.tensor_copy(stt[:, 0:8], ps_lg[:, 256:264])
                        vector.drain()
                        vector.tensor_scalar(stt[:, 0:4], stt[:, 0:4], 1.0 / D, None, ALU.mult)
                        vector.tensor_scalar(stt[:, 4:8], stt[:, 4:8], 1.0 / D, None, ALU.mult)
                        vector.drain()
                        vector.tensor_tensor(stt[:, 12:16], stt[:, 0:4], stt[:, 0:4], ALU.mult)
                        vector.drain()
                        vector.tensor_tensor(stt[:, 4:8], stt[:, 4:8], stt[:, 12:16],
                                             ALU.subtract)
                        vector.drain().then_inc(s_ast, 1)
                        vector.wait_ge(s_arw, B_arw + 2 * lr + 1)
                        vector.tensor_copy(xn_f32[0:1, 0:512], ps_pos[0:1, 0:512])
                        vector.tensor_copy(xn_f32[0:1, 512:1024], ps_m2[0:1, 0:512])
                        vector.drain().then_inc(s_ast, 1)
                        vector.wait_ge(s_arw, B_arw + 2 * lr + 2)
                        vector.tensor_copy(MUB[:, 0:512], ps_pos[:, 0:512])
                        vector.tensor_copy(RSB[:, 0:512], ps_m2[:, 0:512])
                        vector.drain()
                        for dc in range(8):
                            vector.tensor_tensor(
                                bass.AP(SQ.tensor, SQ.offset + dc * 512,
                                        [[SQ.ap[0][0], 128], [1, 512]]),
                                bass.AP(XS.tensor, XS.offset + dc * 512,
                                        [[XS.ap[0][0], 128], [1, 512]]),
                                MUB[:, 0:512], ALU.subtract)
                        vector.drain()
                        for dc in range(8):
                            vector.tensor_tensor(
                                bass.AP(SQ.tensor, SQ.offset + dc * 512,
                                        [[SQ.ap[0][0], 128], [1, 512]]),
                                bass.AP(SQ.tensor, SQ.offset + dc * 512,
                                        [[SQ.ap[0][0], 128], [1, 512]]),
                                RSB[:, 0:512], ALU.mult)
                        vector.drain()
                        for dc in range(8):
                            vector.tensor_scalar(
                                bass.AP(SQ.tensor, SQ.offset + dc * 512,
                                        [[SQ.ap[0][0], 128], [1, 512]]),
                                bass.AP(SQ.tensor, SQ.offset + dc * 512,
                                        [[SQ.ap[0][0], 128], [1, 512]]),
                                gam_s[:, dc:dc + 1], bet_s[:, dc:dc + 1],
                                ALU.mult, ALU.add)
                        if r == 0:
                            vector.drain().then_inc(s_ax2, 1)
                            for pss in range(2):
                                vector.wait_ge(s_aw2c, B_aw2c + tg * 64 + pss * 32 + 32)
                                for dq in range(4):
                                    dch = pss * 4 + dq
                                    dstb = [ps_m0, ps_m1, ps_m2, ps_m3][dq]
                                    vector.scalar_tensor_tensor(
                                        bass.AP(XS.tensor, XS.offset + dch * 512,
                                                [[XS.ap[0][0], 128], [1, 512]]),
                                        dstb[:, 0:512], b2a_s[:, dch:dch + 1],
                                        bass.AP(SQ.tensor, SQ.offset + dch * 512,
                                                [[SQ.ap[0][0], 128], [1, 512]]),
                                        ALU.add, ALU.add)
                                vector.drain().then_inc(s_axe, 1)
                        else:
                            vector.drain().then_inc(s_ax4, 1)
                    for tt in range(4):
                        gt = tg * 4 + tt
                        vector.wait_ge(s_a4t, B_a4t + 2 * gt + 2)
                        if gt >= 2:
                            vector.wait_ge(s_o1[gt % 2], 16 * ((gt - 2) // 2 + 1))
                        xres = (xgl[:, tt * D:(tt + 1) * D] if tt < 2
                                else xn_tm[:, (tt - 2) * D:(tt - 1) * D])
                        OUT = ew1_sb[:, OUTC + (gt % 2) * 2048:OUTC + (gt % 2) * 2048 + 2048].bitcast(F32)
                        for h2 in range(2):
                            src = ps_tr if h2 == 0 else ps_tr2
                            vector.tensor_tensor(OUT[:, h2 * 512:(h2 + 1) * 512],
                                                 xres[:, h2 * 512:(h2 + 1) * 512],
                                                 src[:], ALU.add)
                        vector.drain().then_inc(s_a4v, 1)

            vector.wait_ge(s_cc, CC_YAG)
            for i in range(NT):
                vector.wait_ge(s_yg[i % 2], 32 * (i // 2 + 1))
                if i >= 1 and lv >= 9:
                    vector.wait_ge(s_moA, 16 * i)
                mt = xgl[:, (i % 2) * D:(i % 2 + 1) * D]
                vector.tensor_scalar(mt, y_fm[:, (i % 2) * D:(i % 2 + 1) * D],
                                     g0_a[:, i:i + 1], None, ALU.mult)
                vector.drain()
                vector.scalar_tensor_tensor(y_tm[:, 0:D],
                                            xgb[:, (i % 2) * D:(i % 2 + 1) * D],
                                            g1_a[:, i:i + 1], mt, ALU.mult, ALU.add)
                vector.drain().then_inc(s_cmb, 1)

        @block.tensor
        def _(tensor):
            tensor.wait_ge(s_const, 16 * NCONST)
            for i in range(NT):
                p2 = i % 2
                tensor.wait_ge(s_xn, i + 1)
                if i >= 1:
                    tensor.wait_ge(s_evf, 2 * i)
                for h2 in range(2):
                    dst = ps_tr if h2 == 0 else ps_tr2
                    for q in range(4):
                        ch = h2 * 4 + q
                        ins = tensor.transpose(
                            dst[:, q * 128:(q + 1) * 128],
                            xn_tm[:, p2 * D + ch * 128: p2 * D + (ch + 1) * 128],
                            ident[:])
                    ins.then_inc(s_tr, 1)
                tensor.wait_ge(s_evf, 2 * i + 2)
                if i >= 2:
                    tensor.wait_ge(s_lgv, i - 1)
                for ch in range(NCH):
                    ins = tensor.matmul(
                        ps_lg[:, (i % 2) * 16:(i % 2) * 16 + 8],
                        xn_f32[:, p2 * D + ch * 128: p2 * D + (ch + 1) * 128],
                        wg_sb[:, ch * 8:(ch + 1) * 8],
                        start=(ch == 0), stop=(ch == NCH - 1))
                ins.then_inc(s_lgm, 1)
            tensor.wait_ge(s_t2, 1)
            for i in range(NT):
                ins = tensor.matmul(ps_lg[0:16, 32 + i:33 + i],
                                    oh_all[:, i * 16:(i + 1) * 16], onesc[:],
                                    start=True, stop=True)
            ins.then_inc(s_tot, 1)
            tensor.wait_ge(s_mg, 160)
            tensor.matmul(ps_lg[0:1, 64:80], cntg[:], basemat[:],
                          start=True, stop=True).then_inc(s_mt, 1)
            tensor.wait_ge(s_mid, 1)
            tensor.transpose(ps_pos[0:NT, 96:112], tot_sb[:],
                             ident[0:16, 0:16]).then_inc(s_mt, 1)
            tensor.wait_ge(s_mid, 3)
            tensor.matmul(ps_pos[0:NT, 64:80], lstrict[0:NT, 0:NT], totT_sb[:],
                          start=True, stop=False)
            tensor.matmul(ps_pos[0:NT, 64:80], ones1[0:1, 0:NT], basef[:],
                          start=False, stop=True).then_inc(s_mt, 1)
            tensor.wait_ge(s_mg, 192)
            tensor.wait_ge(s_mid, 4)
            for i in range(NT):
                tensor.matmul(ps_pos[:, i * 16:(i + 1) * 16],
                              ones1[:, 0:128], fns_a[0:1, i * 16:(i + 1) * 16],
                              start=True, stop=False)
                ins = tensor.matmul(ps_pos[:, i * 16:(i + 1) * 16],
                                    lstrict[:], oh_all[:, i * 16:(i + 1) * 16],
                                    start=False, stop=True)
            ins.then_inc(s_fp, 1)

            if lv >= 2:
                tensor.wait_ge(s_ebf, NT)
                tensor.wait_ge(s_wk, 48)
                tensor.wait_ge(s_fv, 1)
            KVB = [ps_m0, ps_m1, ps_m2, ps_m3, ps_pos]
            for tg in range(8 if lv >= 2 else 0):
                for oc in range(8):
                    ktg = tg * 8 + oc
                    if ktg >= 1:
                        tensor.wait_ge(s_akv, ktg)
                    for dch in range(NCH):
                        ins = tensor.matmul(
                            ps_tr[:, 0:512],
                            ew2_sb[:, (dch * 16 + oc) * 128:(dch * 16 + oc + 1) * 128],
                            xn_fm[:, dch * TPC + tg * 512:dch * TPC + (tg + 1) * 512],
                            start=(dch == 0), stop=(dch == NCH - 1))
                    ins.then_inc(s_akm, 1)
                for tt in range(4):
                    gtt = tg * 4 + tt
                    for vh in range(2):
                        idx_v = gtt * 2 + vh
                        if idx_v >= 1:
                            tensor.wait_ge(s_avv, idx_v)
                        for dch in range(NCH):
                            ins = tensor.matmul(
                                ps_tr2[:, 0:512],
                                xn_fm[:, dch * TPC + gtt * 128:dch * TPC + (gtt + 1) * 128],
                                ew2_sb[:, (dch * 16 + 8 + 4 * vh) * 128:
                                       (dch * 16 + 12 + 4 * vh) * 128],
                                start=(dch == 0), stop=(dch == NCH - 1))
                        ins.then_inc(s_avm, 1)
                for tt in range(4):
                    gtt = tg * 4 + tt
                    for h in range(16):
                        kp = gtt * 16 + h
                        tensor.wait_ge(s_akv, tg * 8 + h // 2 + 1)
                        if kp >= 2:
                            tensor.wait_ge(s_akf, kp - 1)
                        tensor.matmul(
                            ps_lg[:, (h % 2) * 256:(h % 2) * 256 + 256],
                            ew2_sb[(h % 2) * 64:(h % 2) * 64 + 64,
                                   KSB0 + (h // 2) * 512 + tt * 128:
                                   KSB0 + (h // 2) * 512 + (tt + 1) * 128],
                            ew2_sb[(h % 2) * 64:(h % 2) * 64 + 64,
                                   PRJ0 + h * 256:PRJ0 + (h + 1) * 256],
                            start=True, stop=True).then_inc(s_akp, 1)
                        tensor.wait_ge(s_akf, kp + 1)
                        tensor.wait_ge(s_avv, gtt * 2 + 2)
                        if gtt >= 1 and h == 0:
                            tensor.wait_ge(s_kva, gtt)
                        for fh in range(2):
                            j = h * 2 + fh
                            tensor.matmul(
                                KVB[j // 8][:, (j % 8) * 64:(j % 8) * 64 + 64],
                                ew2_sb[:, KFB0 + (kp % 2) * 256 + fh * 128:
                                       KFB0 + (kp % 2) * 256 + (fh + 1) * 128],
                                ew2_sb[:, V650 + (gtt % 4) * 1040 + h * 65:
                                       V650 + (gtt % 4) * 1040 + h * 65 + 64],
                                start=True, stop=True)
                            ins = tensor.matmul(
                                ps_pos[:, j:j + 1],
                                ew2_sb[:, KFB0 + (kp % 2) * 256 + fh * 128:
                                       KFB0 + (kp % 2) * 256 + (fh + 1) * 128],
                                ew2_sb[:, V650 + (gtt % 4) * 1040 + h * 65 + 64:
                                       V650 + (gtt % 4) * 1040 + h * 65 + 65],
                                start=True, stop=True)
                            ins.then_inc(s_akk, 1)

            tensor.wait_ge(s_ew, 32)
            for st in range(NST):
                g = st // 2
                tensor.wait_ge(s_cst, st + 1)
                if st >= 1:
                    tensor.wait_ge(s_evB, st)
                pb = ps_tr[:].bitcast(BF16)
                for ch in range(NCH):
                    ins = tensor.transpose(
                        bass.AP(pb.tensor, pb.offset + ch * 128,
                                [[pb.ap[0][0], 128], [1, 128]]),
                        xgb[:, (st % 2) * D + ch * 128:(st % 2) * D + (ch + 1) * 128],
                        identb[:])
                ins.then_inc(s_trB, 1)
                if st % 2 == 1:
                    tensor.wait_ge(s_evB, st + 1)
                    dsp0 = disp_fm.ap().offset + (g % 2) * 2048
                    for hch in range(32):
                        if g * 32 + hch >= 2:
                            tensor.wait_ge(s_hev, g * 32 + hch - 1)
                        dst = ps_lg if hch % 2 == 0 else ps_pos
                        for dch in range(NCH):
                            ins = tensor.matmul(
                                dst[:, 0:256],
                                ew1_sb[:, dch * 4096 + hch * 128:
                                       dch * 4096 + (hch + 1) * 128],
                                bass.AP(disp_fm.ap().tensor, dsp0 + dch * 256,
                                        [[disp_fm.ap().ap[0][0], 128], [1, 256]]),
                                start=(dch == 0), stop=(dch == NCH - 1))
                        ins.then_inc(s_hm, 1)
                    for pss in range(2):
                        if g * 2 + pss >= 1:
                            tensor.wait_ge(s_yev, g * 2 + pss)
                        for hch in range(32):
                            tensor.wait_ge(s_hev, g * 32 + hch + 1)
                            for dq in range(4):
                                dch = pss * 4 + dq
                                dstb = [ps_m0, ps_m1, ps_m2, ps_m3][dq]
                                ins = tensor.matmul(
                                    dstb[:, 0:256],
                                    ew2_sb[:, hch * D + dch * 128:
                                           hch * D + (dch + 1) * 128],
                                    h_fm[:, hch * 256:(hch + 1) * 256],
                                    start=(hch == 0), stop=(hch == 31))
                        ins.then_inc(s_ya, 1)
                    for q in range(2):
                        st2 = g * 2 + q
                        tensor.wait_ge(s_yev, g * 2 + 2)
                        if st2 >= 1:
                            tensor.wait_ge(s_ytv, st2)
                        pb2 = ps_tr2[:].bitcast(BF16)
                        for dch in range(NCH):
                            ins = tensor.transpose(
                                bass.AP(pb2.tensor, pb2.offset + dch * 128,
                                        [[pb2.ap[0][0], 128], [1, 128]]),
                                y_fm[:, dch * 256 + q * 128:dch * 256 + (q + 1) * 128],
                                identb[:])
                        ins.then_inc(s_ytr, 1)


            if lv >= 9:
                tensor.wait_ge(s_kvl, 16)
                tensor.wait_ge(s_wq2, 32)
                tensor.wait_ge(s_kvb, 1)
                KVBF = 4160
                for tg in range(8):
                    tgp = tg % 2
                    if tg >= 1:
                        tensor.wait_ge(s_a4v, B_a4v + 4 * tg)
                        tensor.wait_ge(s_ax4, tg)
                    for tt in range(4):
                        gt = tg * 4 + tt
                        tensor.wait_ge(s_axn, B_axn + gt + 1)
                        if gt >= 1:
                            tensor.wait_ge(s_aev, B_aev + 2 * gt)
                        for h2 in range(2):
                            dst = ps_tr if h2 == 0 else ps_tr2
                            for q in range(4):
                                ch = h2 * 4 + q
                                ins = tensor.transpose(
                                    dst[:, q * 128:(q + 1) * 128],
                                    xn_f32[:, (gt % 2) * D + ch * 128:
                                           (gt % 2) * D + (ch + 1) * 128],
                                    ident[:])
                            ins.then_inc(s_atr, 1)
                    for qc in range(8):
                        u = tg * 8 + qc
                        tensor.wait_ge(s_awq[u % 2], 16 * (u // 2 + 1))
                        tensor.wait_ge(s_aev, B_aev + 8 * (tg + 1))
                        if u >= 1:
                            tensor.wait_ge(s_aqv, B_aqv + u)
                        for dch in range(NCH):
                            ins = tensor.matmul(
                                ps_m2[:, 0:512],
                                ew2_sb[:, WQB0 + (u % 2) * 1024 + dch * 128:
                                       WQB0 + (u % 2) * 1024 + (dch + 1) * 128],
                                disp_fm[:, dch * 512:(dch + 1) * 512],
                                start=(dch == 0), stop=(dch == NCH - 1))
                        ins.then_inc(s_aqm, 1)
                        if qc % 2 == 0 and u >= 2:
                            tensor.wait_ge(s_aov, B_aov + u - 1)
                        for hq in range(2):
                            h = qc * 2 + hq
                            hh = tg * 16 + h
                            tensor.wait_ge(s_aqv, B_aqv + u + 1)
                            for fh in range(2):
                                c_qf = hh * 2 + fh
                                if c_qf >= 1:
                                    tensor.wait_ge(s_aqe, B_aqe + c_qf)
                                tensor.matmul(
                                    ps_m3[:, 0:512],
                                    ew2_sb[hq * 64:hq * 64 + 64,
                                           PRJ2 + h * 256 + fh * 128:
                                           PRJ2 + h * 256 + (fh + 1) * 128],
                                    ew1_sb[hq * 64:hq * 64 + 64,
                                           QSC + (u % 2) * 512:QSC + (u % 2 + 1) * 512],
                                    start=True, stop=True).then_inc(s_aqf, 1)
                            tensor.wait_ge(s_aqe, B_aqe + 2 * hh + 2)
                            for fh in range(2):
                                j = h * 2 + fh
                                ins = tensor.matmul(
                                    ps_lg[0:1, 0:512],
                                    h_fm[:, KVBF + 2048 + j:KVBF + 2048 + j + 1],
                                    ew1_sb[:, QFC + fh * 512:QFC + (fh + 1) * 512],
                                    start=(fh == 0), stop=(fh == 1))
                            ins.then_inc(s_azm, 1)
                            tensor.wait_ge(s_azv, B_azv + hh + 1)
                            tensor.matmul(ps_lg[:, 0:512], ones1[:, 0:128],
                                          y_tm[:].bitcast(F32)[0:1, 0:512],
                                          start=True, stop=True).then_inc(s_azb, 1)
                            tensor.wait_ge(s_aqz, B_aqz + hh + 1)
                            bank = ps_m0 if qc % 2 == 0 else ps_m1
                            for fh in range(2):
                                j = h * 2 + fh
                                ins = tensor.matmul(
                                    bank[hq * 64:hq * 64 + 64, 0:512],
                                    h_fm[:, KVBF + j * 64:KVBF + (j + 1) * 64],
                                    ew1_sb[:, QFC + fh * 512:QFC + (fh + 1) * 512],
                                    start=(fh == 0), stop=(fh == 1))
                            ins.then_inc(s_aam, 1)
                    for dc in range(8):
                        u = tg * 8 + dc
                        tensor.wait_ge(s_aov, B_aov + tg * 8 + 8)
                        tensor.wait_ge(s_awo[0], B_awo + 16 * (u + 1))
                        if u >= 2:
                            tensor.wait_ge(s_axv, B_axv + u - 1)
                        bank = ps_m2 if dc % 2 == 0 else ps_m3
                        for oc in range(8):
                            ins = tensor.matmul(
                                bank[:, 0:512],
                                ew2_sb[:, WOB0 + (u % 2) * 1024 + oc * 128:
                                       WOB0 + (u % 2) * 1024 + (oc + 1) * 128],
                                ew1_sb[:, AOC + oc * 512:AOC + (oc + 1) * 512],
                                start=(oc == 0), stop=(oc == 7))
                        ins.then_inc(s_axm, 1)
                    for r in range(2):
                        XS = ew1_sb[:, 0:8192].bitcast(F32)
                        SQ = ew1_sb[:, 8192:16384].bitcast(F32)
                        if r == 0:
                            tensor.wait_ge(s_axv, B_axv + tg * 8 + 8)
                        else:
                            tensor.wait_ge(s_axe, B_axe + tg * 2 + 2)
                        tensor.wait_ge(s_asq, B_asq + tg * 16 + r * 8 + 8)
                        for tt in range(4):
                            for dch in range(NCH):
                                ins = tensor.matmul(
                                    ps_lg[:, 256 + tt:257 + tt], 
                                    bass.AP(XS.tensor, XS.offset + dch * 512 + tt * 128,
                                            [[XS.ap[0][0], 128], [1, 128]]),
                                    onesc[:], start=(dch == 0), stop=(dch == NCH - 1))
                            ins.then_inc(s_asm, 0) if False else None
                            for dch in range(NCH):
                                ins = tensor.matmul(
                                    ps_lg[:, 260 + tt:261 + tt],
                                    bass.AP(SQ.tensor, SQ.offset + dch * 512 + tt * 128,
                                            [[SQ.ap[0][0], 128], [1, 128]]),
                                    onesc[:], start=(dch == 0), stop=(dch == NCH - 1))
                        ins.then_inc(s_asm, 1)
                        tensor.wait_ge(s_ars, B_ars + tg * 2 + r + 1)
                        for tt in range(4):
                            tensor.transpose(ps_pos[0:1, tt * 128:(tt + 1) * 128],
                                             stt[:, tt:tt + 1], ident[:])
                            ins = tensor.transpose(ps_m2[0:1, tt * 128:(tt + 1) * 128],
                                                   stt[:, 8 + tt:9 + tt], ident[:])
                        ins.then_inc(s_arw, 1)
                        tensor.wait_ge(s_ast, B_ast + (tg * 2 + r) * 2 + 2)
                        tensor.matmul(ps_pos[:, 0:512], ones1[:, 0:128],
                                      xn_f32[0:1, 0:512], start=True, stop=True)
                        ins = tensor.matmul(ps_m2[:, 0:512], ones1[:, 0:128],
                                            xn_f32[0:1, 512:1024], start=True, stop=True)
                        ins.then_inc(s_arw, 1)
                        if r == 0:
                            tensor.wait_ge(s_a2b, B_a2b + tg * 8 + 8)
                            for hch in range(32):
                                uu = tg * 32 + hch
                                tensor.wait_ge(s_aw1[0], B_aw1 + 16 * (uu + 1))
                                if uu >= 2:
                                    tensor.wait_ge(s_ahe, B_ahe + uu - 1)
                                dsth = ps_lg if hch % 2 == 0 else ps_pos
                                for dch in range(NCH):
                                    ins = tensor.matmul(
                                        dsth[:, 0:512],
                                        ew2_sb[:, W1B0 + (uu % 2) * 1024 + dch * 128:
                                               W1B0 + (uu % 2) * 1024 + (dch + 1) * 128],
                                        ew1_sb[:, X2BC + dch * 512:X2BC + (dch + 1) * 512],
                                        start=(dch == 0), stop=(dch == NCH - 1))
                                ins.then_inc(s_ahm, 1)
                            for pss in range(2):
                                for hch in range(32):
                                    uu = tg * 64 + pss * 32 + hch
                                    tensor.wait_ge(s_ahe, B_ahe + tg * 32 + hch + 1)
                                    tensor.wait_ge(s_aw2[uu % 2], 16 * (uu // 2 + 1))
                                    if pss == 0 and hch == 0 and tg >= 1:
                                        tensor.wait_ge(s_axe, B_axe + tg * 2)
                                    if pss == 1 and hch == 0:
                                        tensor.wait_ge(s_axe, B_axe + tg * 2 + 1)
                                    for dq in range(4):
                                        dch = pss * 4 + dq
                                        dstb = [ps_m0, ps_m1, ps_m2, ps_m3][dq]
                                        ins = tensor.matmul(
                                            dstb[:, 0:512],
                                            ew2_sb[:, W2B0 + (uu % 2) * 1024 + dch * 128:
                                                   W2B0 + (uu % 2) * 1024 + (dch + 1) * 128],
                                            ew2_sb[:, H2A0 + hch * 512:H2A0 + (hch + 1) * 512],
                                            start=(hch == 0), stop=(hch == 31))
                                    ins.then_inc(s_aw2c, 1)
                    tensor.wait_ge(s_ax4, B_ax4 + tg + 1)
                    for tt in range(4):
                        gt = tg * 4 + tt
                        if gt >= 1:
                            tensor.wait_ge(s_a4v, B_a4v + gt)
                        X4 = ew1_sb[:, 8192:16384].bitcast(F32)
                        for h2 in range(2):
                            dst = ps_tr if h2 == 0 else ps_tr2
                            for q in range(4):
                                dc = h2 * 4 + q
                                ins = tensor.transpose(
                                    dst[:, q * 128:(q + 1) * 128],
                                    bass.AP(X4.tensor, X4.offset + dc * 512 + tt * 128,
                                            [[X4.ap[0][0], 128], [1, 128]]),
                                    ident[:])
                            ins.then_inc(s_a4t, 1)

        @block.gpsimd
        def _(gpsimd):
            bc_slot = gpsimd.alloc_register()
            gpsimd.reg_mov(bc_slot, NSLOT - 1)
            wkv_src = bass.AP(wqkv_l.ap().tensor, wqkv_l.ap().offset + 8 * 128,
                              [[24 * 128 * 8, 128], [24 * 128, NCH], [1, 16 * 128]])
            gpsimd.dma_start(out=ew2_sb[:, 0:16384], in_=wkv_src).then_inc(s_wk, 16)
            gpsimd.dma_start(out=ew2_sb[0:64, PRJ0:PRJ0 + 4096],
                             in_=proj_l[:, :]).then_inc(s_wk, 16)
            gpsimd.dma_start(out=ew2_sb[64:128, PRJ0:PRJ0 + 4096],
                             in_=proj_l[:, :]).then_inc(s_wk, 16)
            wq_src = bass.AP(wqkv_l.ap().tensor, wqkv_l.ap().offset,
                             [[24 * 128 * 8, 128], [24 * 128, NCH], [1, 8 * 128]])
            gpsimd.dma_start(out=wq_bf[:, :], in_=wq_src).then_inc(s_wk, 16)
            gpsimd.dma_start(out=wout_bf[:, :], in_=wout_l[:, :]).then_inc(s_wk, 16)
            gpsimd.dma_start(out=w1a_bf[:, :], in_=w1a_l[:, :]).then_inc(s_wk, 16)
            gpsimd.dma_start(out=w2a_bf[:, :], in_=w2a_l[:, :]).then_inc(s_wk, 16)
            bc_tok = gpsimd.alloc_register()
            gpsimd.reg_mov(bc_tok, NTOK - 1)
            bc_yall = gpsimd.alloc_register()
            gpsimd.reg_mov(bc_yall, NSLOT - 1)
            gpsimd.memset(myrec[:, 0:256], 0.0)
            gpsimd.drain()
            rcv = rec_c.ap().rearrange("(a p c2) c -> a p (c2 c)", a=8, p=128)
            for a in range(8):
                gpsimd.dma_start(out=rcv[a], in_=myrec[:, 0:256]).then_inc(s_mg, 16)
            gpsimd.wait_ge(s_mid, 1)
            gpsimd.dma_start(out=cnt_in[:, :], in_=cnt_sb[:]).then_inc(s_mg, 16)
            gpsimd.wait_ge(s_mg, 144)
            gpsimd.collective_compute(
                "AllGather", ALU.bypass, replica_groups=RG,
                ins=[cnt_in.ap().opt()], outs=[cnt_out.ap().opt()],
            ).then_inc(s_cc, 1)
            gpsimd.wait_ge(s_cc, 1)
            gpsimd.dma_start(out=cntg[:], in_=cnt_out[:, :]).then_inc(s_mg, 16)
            gpsimd.wait_ge(s_mid, 4)
            gpsimd.dma_start(
                out=runrow_d.ap().rearrange("o (a b) -> (o a) b", a=NT, b=16),
                in_=runp_sb[:]).then_inc(s_mg, 16)
            gpsimd.wait_ge(s_mg, 176)
            gpsimd.dma_start(out=fns_a[0:1, 0:NT * 16], in_=runrow_d[0:1, :]).then_inc(s_mg, 16)
            gpsimd.wait_ge(s_fv, 1)
            for i in range(NT):
                for k in range(2):
                    gpsimd.indirect_dma_start(
                        out=rec_c[:, :],
                        out_offset=IndirectOffsetOnAxis(
                            ap=sidx_a[:, i * 2 + k:i * 2 + k + 1], axis=0),
                        in_=lg_all[:, i * 8 + 4 * k:i * 8 + 4 * k + 4],
                        in_offset=None,
                        bounds_check=bc_slot, oob_is_err=False,
                    ).then_inc(s_sc, 16)
            gpsimd.wait_ge(s_sc, 16 * 2 * NT)
            gpsimd.collective_compute(
                "AllReduce", ALU.add, replica_groups=RG,
                ins=[rec_c.ap().opt()], outs=[rec_all.ap().opt()],
            ).then_inc(s_cc, 1)
            gpsimd.wait_ge(s_cc, 2)
            for sti in range(NST):
                gpsimd.indirect_dma_start(
                    out=myrec[:, sti * 4:(sti + 1) * 4],
                    out_offset=None,
                    in_=rec_all[:, :],
                    in_offset=IndirectOffsetOnAxis(
                        ap=gidxc[:, sti:sti + 1], axis=0),
                    bounds_check=None, oob_is_err=False,
                ).then_inc(s_gr, 16)
            if lv >= 2:
                gpsimd.wait_ge(s_akm, 8 * 8)
                gpsimd.wait_ge(s_avm, 8 * 8)
                gpsimd.wait_ge(s_akk, 2 * 16 * NT)
                gpsimd.wait_ge(s_akvd, 1)
            else:
                gpsimd.wait_ge(s_ebf, NT)
            gpsimd.dma_start(out=ew1_sb[:], in_=ew1[:, :]).then_inc(s_ew, 16)
            gpsimd.dma_start(out=ew2_sb[:], in_=ew2[:, :]).then_inc(s_ew, 16)
            gpsimd.wait_ge(s_tix, 1)
            for st in range(NST):
                if st >= 2:
                    gpsimd.wait_ge(s_cst, st - 1)
                gpsimd.indirect_dma_start(
                    out=xgl[:, (st % 2) * D:(st % 2 + 1) * D],
                    out_offset=None,
                    in_=xf[:, :],
                    in_offset=IndirectOffsetOnAxis(
                        ap=tidx_a[:, st:st + 1], axis=0),
                    bounds_check=bc_tok, oob_is_err=False,
                ).then_inc(s_xg[st % 2], 16)
            if lv >= 2:
                gpsimd.wait_ge(s_kvin, 16)
                gpsimd.collective_compute(
                    "AllReduce", ALU.add, replica_groups=RGP,
                    ins=[kv_in.ap().opt()], outs=[kv_out.ap().opt()],
                ).then_inc(s_cc, 1)
            if lv >= 9:
                gpsimd.wait_ge(s_ya, 64)
                gpsimd.dma_start(out=h_fm[:, 0:4160].bitcast(F32),
                                 in_=kv_out[:, :]).then_inc(s_kvl, 16)
                gpsimd.dma_start(out=ew2_sb[0:64, PRJ2:PRJ2 + 4096],
                                 in_=proj_l[:, :]).then_inc(s_wq2, 16)
                gpsimd.dma_start(out=ew2_sb[64:128, PRJ2:PRJ2 + 4096],
                                 in_=proj_l[:, :]).then_inc(s_wq2, 16)
            gpsimd.wait_ge(s_ym[0], 16 * (NST // 2))
            gpsimd.wait_ge(s_ym[1], 16 * (NST // 2))
            gpsimd.collective_compute(
                "AllGather", ALU.bypass, replica_groups=RG,
                ins=[y_mine.ap().opt()], outs=[y_all.ap().opt()],
            ).then_inc(s_cc, 1)
            gpsimd.wait_ge(s_cc, CC_YAG)
            gpsimd.wait_ge(s_fv, 1)
            for i in range(NT):
                if i >= 2:
                    gpsimd.wait_ge(s_cmb, i - 1)
                    if lv >= 9:
                        gpsimd.wait_ge(s_o1[(i - 2) % 2], 16 * ((i - 2) // 2 + 1))
                        gpsimd.dma_start(out=out_d[(i - 2) * 128:(i - 1) * 128, :],
                                         in_=y_tm[:, 0:D],
                                         accum_op=ALU.add).then_inc(s_moA, 16)
                gpsimd.indirect_dma_start(
                    out=y_fm[:, (i % 2) * D:(i % 2 + 1) * D],
                    out_offset=None,
                    in_=y_all[:, :],
                    in_offset=IndirectOffsetOnAxis(
                        ap=yidx_a[:, 2 * i:2 * i + 1], axis=0),
                    bounds_check=bc_yall, oob_is_err=False,
                ).then_inc(s_yg[i % 2], 16)
                gpsimd.indirect_dma_start(
                    out=xgb[:, (i % 2) * D:(i % 2 + 1) * D],
                    out_offset=None,
                    in_=y_all[:, :],
                    in_offset=IndirectOffsetOnAxis(
                        ap=yidx_a[:, 2 * i + 1:2 * i + 2], axis=0),
                    bounds_check=bc_yall, oob_is_err=False,
                ).then_inc(s_yg[i % 2], 16)
            for i in ([NT - 2, NT - 1] if lv >= 9 else []):
                gpsimd.wait_ge(s_cmb, i + 1)
                gpsimd.wait_ge(s_o1[i % 2], 16 * (i // 2 + 1))
                gpsimd.dma_start(out=out_d[i * 128:(i + 1) * 128, :],
                                 in_=y_tm[:, 0:D],
                                 accum_op=ALU.add).then_inc(s_moA, 16)

    return nc



def _build_inmaps(inputs):
    import ml_dtypes
    x = np.ascontiguousarray(
        np.asarray(inputs["L0_attn_x"], np.float32).reshape(-1, D))
    wg0 = np.asarray(inputs["L0_ffn_wg"], np.float32)
    wg = np.ascontiguousarray(
        wg0.reshape(NCH, 128, E).transpose(1, 0, 2).reshape(128, NCH * E))
    w_qkv = np.asarray(inputs["L0_attn_w_qkv"], np.float32)
    proj = np.asarray(inputs["L0_attn_proj"], np.float32)
    w_out = np.asarray(inputs["L0_attn_w_out"], np.float32)
    b_out = np.asarray(inputs["L0_attn_b_out"], np.float32)
    gamma = np.asarray(inputs["L0_attn_gamma"], np.float32)
    beta = np.asarray(inputs["L0_attn_beta"], np.float32)
    w1a = np.asarray(inputs["L0_attn_w1"], np.float32)
    b1a = np.asarray(inputs["L0_attn_b1"], np.float32)
    w2a = np.asarray(inputs["L0_attn_w2"], np.float32)
    b2a = np.asarray(inputs["L0_attn_b2"], np.float32)
    fw1 = np.asarray(inputs["L0_ffn_w1"], np.float32)
    fb1 = np.asarray(inputs["L0_ffn_b1"], np.float32)
    fw2 = np.asarray(inputs["L0_ffn_w2"], np.float32)
    fb2 = np.asarray(inputs["L0_ffn_b2"], np.float32)

    wqkv_l = np.ascontiguousarray(
        w_qkv.reshape(24, 128, 8, 128).transpose(3, 2, 0, 1).reshape(128, 24576))
    proj_l = np.ascontiguousarray(proj.transpose(1, 0, 2).reshape(64, 4096))
    wout_l = np.ascontiguousarray(
        w_out.reshape(8, 128, 8, 128).transpose(3, 0, 2, 1).reshape(128, 8192))
    w1a_l = np.ascontiguousarray(
        w1a.reshape(32, 128, 8, 128).transpose(3, 0, 2, 1).reshape(128, 32768))
    w2a_l = np.ascontiguousarray(
        w2a.reshape(8, 128, 32, 128).transpose(3, 2, 0, 1).reshape(128, 32768))

    in_maps = []
    for c in range(8):
        m = dict(xf=x, xm=x[c * TPC:(c + 1) * TPC], wg=wg,
                 wqkv_l=wqkv_l, proj_l=proj_l, wout_l=wout_l,
                 w1a_l=w1a_l, w2a_l=w2a_l,
                 bout_c=np.ascontiguousarray(b_out.reshape(8, 128).T),
                 gam_c=np.ascontiguousarray(gamma.reshape(8, 128).T),
                 bet_c=np.ascontiguousarray(beta.reshape(8, 128).T),
                 b1a_c=np.ascontiguousarray(b1a.reshape(32, 128).T),
                 b2a_c=np.ascontiguousarray(b2a.reshape(8, 128).T))
        m["ew1"] = np.ascontiguousarray(
            fw1[c].reshape(NCH, 128, 4096).transpose(1, 0, 2).reshape(128, NCH * 4096))
        m["ew2"] = np.ascontiguousarray(
            fw2[c].reshape(32, 128, D).transpose(1, 0, 2).reshape(128, 32 * D))
        m["eb1_i"] = np.ascontiguousarray(fb1[c].reshape(32, 128).T)
        m["eb2_i"] = np.ascontiguousarray(fb2[c].reshape(8, 128).T)
        m["identb_i"] = np.eye(128).astype(ml_dtypes.bfloat16)
        m.update(make_const_inputs(c))
        in_maps.append(m)
    return in_maps


_NC_CACHE = {}


def kernel(**inputs):
    from concourse.bass_utils import run_bass_kernel_spmd
    if "C" not in _NC_CACHE:
        _NC_CACHE["C"] = build("C")
    nc = _NC_CACHE["C"]
    in_maps = _build_inmaps(inputs)
    res = run_bass_kernel_spmd(nc, in_maps, core_ids=list(range(8)))
    out = np.concatenate(
        [np.asarray(res.results[c]["out_d"], np.float32) for c in range(8)], axis=0)
    return out.reshape(4, 8192, D)


# version 02 (final)
# speedup vs baseline: 1.0061x; 1.0061x over previous
import numpy as np
import concourse.bass as bass
import concourse.mybir as mybir
from concourse.bass import IndirectOffsetOnAxis

F32 = mybir.dt.float32
BF16 = mybir.dt.bfloat16
U32 = mybir.dt.uint32
AF = mybir.ActivationFunctionType
ALU = mybir.AluOpType
AX = mybir.AxisListType

NCORE = 8
D = 1024
NTOK = 32768
TPC = 4096
NT = TPC // 128
NCH = D // 128
E = 8
CAP = 8192
NSLOT = E * CAP
SPC = NSLOT // NCORE
NST = SPC // 128
OOB = float(1 << 26)
EPS_LN = 1e-5
EPS_ATTN = 1e-8
RG = [[0, 1, 2, 3, 4, 5, 6, 7]]


def view3(t, p_cnt, g, inner, offset=0, gstride=None):
    a = t.ap()
    row = a.ap[0][0]
    gs = inner if gstride is None else gstride
    return bass.AP(a.tensor, a.offset + offset, [[row, p_cnt], [gs, g], [1, inner]])


def bcast3(t, p_cnt, g, inner, offset=0):
    a = t.ap()
    row = a.ap[0][0]
    return bass.AP(a.tensor, a.offset + offset, [[row, p_cnt], [1, g], [0, inner]])


def bcast_mid(t, p_cnt, g, mid, inner, offset=0):
    a = t.ap()
    row = a.ap[0][0]
    return bass.AP(a.tensor, a.offset + offset,
                   [[row, p_cnt], [inner, g], [0, mid], [1, inner]])


def make_const_inputs(core):
    c = {}
    c["ident_i"] = np.eye(128, dtype=np.float32)
    l = np.zeros((128, 128), np.float32)
    for k in range(128):
        l[k, k + 1:] = 1.0
    c["lstrict_i"] = l
    c["iota8_i"] = np.tile(np.arange(8, dtype=np.float32), (128, 1))
    c["ones1_i"] = np.ones((1, 128), np.float32)
    c["onesc_i"] = np.ones((128, 1), np.float32)
    bm = np.zeros((128, 16), np.float32)
    for r in range(NCORE):
        for e in range(8):
            if r < core:
                bm[r * 16 + e, e] = 1.0
                bm[r * 16 + 8 + e, e + 8] = 1.0
            bm[r * 16 + e, e + 8] += 1.0
    c["basemat_i"] = bm
    p = np.arange(128).reshape(128, 1)
    t = np.arange(NT).reshape(1, NT)
    j0 = 2.0 * (core * TPC + t * 128 + p)
    ja = np.zeros((128, NT * 2), np.float32)
    ja[:, 0::2] = j0
    ja[:, 1::2] = j0 + 1
    c["jall_i"] = ja
    st = np.arange(NST).reshape(1, NST)
    c["gidx_i"] = (core * SPC + st * 128 + p).astype(np.uint32)
    c["eps_i"] = np.full((128, 1), EPS_LN, np.float32)
    return c


def build(stage="A"):
    nc = bass.Bass("TRN2", target_bir_lowering=False)

    xf = nc.dram_tensor("xf", [NTOK, D], F32, kind="ExternalInput")
    xm = nc.dram_tensor("xm", [TPC, D], F32, kind="ExternalInput")
    wg = nc.dram_tensor("wg", [128, NCH * 8], F32, kind="ExternalInput")
    ident_i = nc.dram_tensor("ident_i", [128, 128], F32, kind="ExternalInput")
    lstrict_i = nc.dram_tensor("lstrict_i", [128, 128], F32, kind="ExternalInput")
    iota8_i = nc.dram_tensor("iota8_i", [128, 8], F32, kind="ExternalInput")
    ones1_i = nc.dram_tensor("ones1_i", [1, 128], F32, kind="ExternalInput")
    onesc_i = nc.dram_tensor("onesc_i", [128, 1], F32, kind="ExternalInput")
    basemat_i = nc.dram_tensor("basemat_i", [128, 16], F32, kind="ExternalInput")
    jall_i = nc.dram_tensor("jall_i", [128, NT * 2], F32, kind="ExternalInput")
    gidx_i = nc.dram_tensor("gidx_i", [128, NST], U32, kind="ExternalInput")
    eps_i = nc.dram_tensor("eps_i", [128, 1], F32, kind="ExternalInput")

    dbg_rt = nc.dram_tensor("dbg_rt", [128, 8 * NT], F32, kind="ExternalOutput")
    dbg_rec = nc.dram_tensor("dbg_rec", [128, NST * 4], F32, kind="ExternalOutput")

    cnt_in = nc.dram_tensor("cnt_in", [16, 1], F32)
    cnt_out = nc.dram_tensor("cnt_out", [128, 1], F32)
    rec_c = nc.dram_tensor("rec_c", [NSLOT, 4], F32)
    rec_all = nc.dram_tensor("rec_all", [NSLOT, 4], F32)
    runrow_d = nc.dram_tensor("runrow_d", [1, NT * 16], F32)
    ew1 = nc.dram_tensor("ew1", [128, NCH * 4096], F32, kind="ExternalInput")
    ew2 = nc.dram_tensor("ew2", [128, 32 * D], F32, kind="ExternalInput")
    eb1_i = nc.dram_tensor("eb1_i", [128, 32], F32, kind="ExternalInput")
    eb2_i = nc.dram_tensor("eb2_i", [128, 8], F32, kind="ExternalInput")
    identb_i = nc.dram_tensor("identb_i", [128, 128], BF16, kind="ExternalInput")
    wqkv_l = nc.dram_tensor("wqkv_l", [128, 8 * 24 * 128], F32, kind="ExternalInput")
    proj_l = nc.dram_tensor("proj_l", [64, 16 * 256], F32, kind="ExternalInput")
    wout_l = nc.dram_tensor("wout_l", [128, 8 * 8 * 128], F32, kind="ExternalInput")
    w1a_l = nc.dram_tensor("w1a_l", [128, 32 * 8 * 128], F32, kind="ExternalInput")
    w2a_l = nc.dram_tensor("w2a_l", [128, 32 * 8 * 128], F32, kind="ExternalInput")
    bout_c = nc.dram_tensor("bout_c", [128, 8], F32, kind="ExternalInput")
    gam_c = nc.dram_tensor("gam_c", [128, 8], F32, kind="ExternalInput")
    bet_c = nc.dram_tensor("bet_c", [128, 8], F32, kind="ExternalInput")
    b1a_c = nc.dram_tensor("b1a_c", [128, 32], F32, kind="ExternalInput")
    b2a_c = nc.dram_tensor("b2a_c", [128, 8], F32, kind="ExternalInput")
    wq_bf = nc.dram_tensor("wq_bf", [128, 8 * 8 * 128], BF16)
    wout_bf = nc.dram_tensor("wout_bf", [128, 8 * 8 * 128], BF16)
    w1a_bf = nc.dram_tensor("w1a_bf", [128, 32 * 8 * 128], BF16)
    w2a_bf = nc.dram_tensor("w2a_bf", [128, 32 * 8 * 128], BF16)
    a2_ao = nc.dram_tensor("a2_ao", [128, 4096], BF16)
    a2_x3 = nc.dram_tensor("a2_x3", [128, 4096], F32)
    a2_x4 = nc.dram_tensor("a2_x4", [128, 4096], F32)
    a2_xnq = nc.dram_tensor("a2_xnq", [128, 4096], BF16)
    a2_qf = nc.dram_tensor("a2_qf", [128, 1024], BF16)
    kv_in = nc.dram_tensor("kv_in", [128, 32 * 65], F32)
    kv_out = nc.dram_tensor("kv_out", [128, 32 * 65], F32)
    out_d = nc.dram_tensor("out_d", [TPC, D], F32, kind="ExternalOutput")
    RGP = [[0, 1], [2, 3], [4, 5], [6, 7]]
    y_mine = nc.dram_tensor("y_mine", [SPC, D], BF16)
    y_all = nc.dram_tensor("y_all", [NSLOT, D], BF16)

    from contextlib import ExitStack
    es = ExitStack()
    ident = es.enter_context(nc.sbuf_tensor("ident", [128, 128], F32))
    lstrict = es.enter_context(nc.sbuf_tensor("lstrict", [128, 128], F32))
    iota8 = es.enter_context(nc.sbuf_tensor("iota8", [128, 8], F32))
    ones1 = es.enter_context(nc.sbuf_tensor("ones1", [1, 128], F32))
    onesc = es.enter_context(nc.sbuf_tensor("onesc", [128, 1], F32))
    basemat = es.enter_context(nc.sbuf_tensor("basemat", [128, 16], F32))
    jall = es.enter_context(nc.sbuf_tensor("jall", [128, NT * 2], F32))
    gidxc = es.enter_context(nc.sbuf_tensor("gidxc", [128, NST], U32))
    epsc = es.enter_context(nc.sbuf_tensor("epsc", [128, 1], F32))
    wg_sb = es.enter_context(nc.sbuf_tensor("wg_sb", [128, NCH * 8], F32))
    xt = es.enter_context(nc.sbuf_tensor("xt", [128, 2 * D], F32))
    xn_tm = es.enter_context(nc.sbuf_tensor("xn_tm", [128, 2 * D], F32))
    xn_f32 = es.enter_context(nc.sbuf_tensor("xn_f32", [128, 2 * D], F32))
    ew1_sb = es.enter_context(nc.sbuf_tensor("ew1_sb", [128, NCH * 4096], BF16))
    xn_fm = ew1_sb
    ew2_sb = es.enter_context(nc.sbuf_tensor("ew2_sb", [128, 32 * D], BF16))
    eb1_sb = es.enter_context(nc.sbuf_tensor("eb1_sb", [128, 32], F32))
    eb2_sb = es.enter_context(nc.sbuf_tensor("eb2_sb", [128, 8], F32))
    bout_s = es.enter_context(nc.sbuf_tensor("bout_s", [128, 8], F32))
    gam_s = es.enter_context(nc.sbuf_tensor("gam_s", [128, 8], F32))
    bet_s = es.enter_context(nc.sbuf_tensor("bet_s", [128, 8], F32))
    b1a_s = es.enter_context(nc.sbuf_tensor("b1a_s", [128, 32], F32))
    b2a_s = es.enter_context(nc.sbuf_tensor("b2a_s", [128, 8], F32))
    identb = es.enter_context(nc.sbuf_tensor("identb", [128, 128], BF16))
    tidx_a = es.enter_context(nc.sbuf_tensor("tidx_a", [128, NST], U32))
    tidx_f = es.enter_context(nc.sbuf_tensor("tidx_f", [128, 2 * NST], F32))
    yidx_a = es.enter_context(nc.sbuf_tensor("yidx_a", [128, NT * 2], U32))
    xgl = es.enter_context(nc.sbuf_tensor("xgl", [128, 2 * D], F32))
    xgb = es.enter_context(nc.sbuf_tensor("xgb", [128, 2 * D], BF16))
    disp_fm = es.enter_context(nc.sbuf_tensor("disp_fm", [128, 2 * NCH * 256], BF16))
    h_fm = es.enter_context(nc.sbuf_tensor("h_fm", [128, 32 * 256], BF16))
    y_fm = es.enter_context(nc.sbuf_tensor("y_fm", [128, 8 * 256], BF16))
    y_tm = es.enter_context(nc.sbuf_tensor("y_tm", [128, D], BF16))
    sums = es.enter_context(nc.sbuf_tensor("sums", [128, NT], F32))
    ssqs = es.enter_context(nc.sbuf_tensor("ssqs", [128, NT], F32))
    mu_a = es.enter_context(nc.sbuf_tensor("mu_a", [128, NT], F32))
    var_a = es.enter_context(nc.sbuf_tensor("var_a", [128, NT], F32))
    rstd_a = es.enter_context(nc.sbuf_tensor("rstd_a", [128, NT], F32))
    e0_a = es.enter_context(nc.sbuf_tensor("e0_a", [128, NT], F32))
    e1_a = es.enter_context(nc.sbuf_tensor("e1_a", [128, NT], F32))
    m1_a = es.enter_context(nc.sbuf_tensor("m1_a", [128, NT], F32))
    m2_a = es.enter_context(nc.sbuf_tensor("m2_a", [128, NT], F32))
    d0_a = es.enter_context(nc.sbuf_tensor("d0_a", [128, NT], F32))
    d1_a = es.enter_context(nc.sbuf_tensor("d1_a", [128, NT], F32))
    g0_a = es.enter_context(nc.sbuf_tensor("g0_a", [128, NT], F32))
    g1_a = es.enter_context(nc.sbuf_tensor("g1_a", [128, NT], F32))
    pos_a = es.enter_context(nc.sbuf_tensor("pos_a", [128, NT * 2], F32))
    keep_a = es.enter_context(nc.sbuf_tensor("keep_a", [128, NT * 2], F32))
    sidx_f = es.enter_context(nc.sbuf_tensor("sidx_f", [128, NT * 2], F32))
    sidx_a = es.enter_context(nc.sbuf_tensor("sidx_a", [128, NT * 2], U32))
    lg_all = es.enter_context(nc.sbuf_tensor("lg_all", [128, NT * 8], F32))
    oh_all = es.enter_context(nc.sbuf_tensor("oh_all", [128, NT * 16], F32))
    fns_a = es.enter_context(nc.sbuf_tensor("fns_a", [128, NT * 16], F32))
    cnt_sb = es.enter_context(nc.sbuf_tensor("cnt_sb", [16, 1], F32))
    cntg = es.enter_context(nc.sbuf_tensor("cntg", [128, 1], F32))
    tot_sb = es.enter_context(nc.sbuf_tensor("tot_sb", [16, NT], F32))
    totT_sb = es.enter_context(nc.sbuf_tensor("totT_sb", [NT, 16], F32))
    basef = es.enter_context(nc.sbuf_tensor("basef", [1, 16], F32))
    runp_sb = es.enter_context(nc.sbuf_tensor("runp_sb", [NT, 16], F32))
    myrec = es.enter_context(nc.sbuf_tensor("myrec", [128, NST * 4], F32))
    ps_tr = es.enter_context(nc.psum_tensor("ps_tr", [128, 512], F32))
    ps_tr2 = es.enter_context(nc.psum_tensor("ps_tr2", [128, 512], F32))
    ps_lg = es.enter_context(nc.psum_tensor("ps_lg", [128, 512], F32))
    ps_pos = es.enter_context(nc.psum_tensor("ps_pos", [128, 512], F32))
    ps_m0 = es.enter_context(nc.psum_tensor("ps_m0", [128, 512], F32))
    ps_m1 = es.enter_context(nc.psum_tensor("ps_m1", [128, 512], F32))
    ps_m2 = es.enter_context(nc.psum_tensor("ps_m2", [128, 512], F32))
    ps_m3 = es.enter_context(nc.psum_tensor("ps_m3", [128, 512], F32))
    s_l1 = [es.enter_context(nc.semaphore(f"s_l1{j}")) for j in range(2)]
    s_l2 = [es.enter_context(nc.semaphore(f"s_l2{j}")) for j in range(2)]
    s_const = es.enter_context(nc.semaphore("s_const"))
    s_sq = es.enter_context(nc.semaphore("s_sq"))
    s_red = es.enter_context(nc.semaphore("s_red"))
    s_bst = es.enter_context(nc.semaphore("s_bst"))
    s_rst = es.enter_context(nc.semaphore("s_rst"))
    s_xn = es.enter_context(nc.semaphore("s_xn"))
    s_tr = es.enter_context(nc.semaphore("s_tr"))
    s_evf = es.enter_context(nc.semaphore("s_evf"))
    s_ebf = es.enter_context(nc.semaphore("s_ebf"))
    s_lgm = es.enter_context(nc.semaphore("s_lgm"))
    s_lgv = es.enter_context(nc.semaphore("s_lgv"))
    s_t2 = es.enter_context(nc.semaphore("s_t2"))
    s_sig = es.enter_context(nc.semaphore("s_sig"))
    s_tot = es.enter_context(nc.semaphore("s_tot"))
    s_mid = es.enter_context(nc.semaphore("s_mid"))
    s_mg = es.enter_context(nc.semaphore("s_mg"))
    s_cc = es.enter_context(nc.semaphore("s_cc"))
    s_mt = es.enter_context(nc.semaphore("s_mt"))
    s_fp = es.enter_context(nc.semaphore("s_fp"))
    s_fv = es.enter_context(nc.semaphore("s_fv"))
    s_sc = es.enter_context(nc.semaphore("s_sc"))
    s_gr = es.enter_context(nc.semaphore("s_gr"))
    s_out = es.enter_context(nc.semaphore("s_out"))
    s_ew = es.enter_context(nc.semaphore("s_ew"))
    s_tix = es.enter_context(nc.semaphore("s_tix"))
    s_xg = [es.enter_context(nc.semaphore(f"s_xg{j}")) for j in range(2)]
    s_lnv = es.enter_context(nc.semaphore("s_lnv"))
    s_cst = es.enter_context(nc.semaphore("s_cst"))
    s_trB = es.enter_context(nc.semaphore("s_trB"))
    s_evB = es.enter_context(nc.semaphore("s_evB"))
    s_hm = es.enter_context(nc.semaphore("s_hm"))
    s_hev = es.enter_context(nc.semaphore("s_hev"))
    s_ya = es.enter_context(nc.semaphore("s_ya"))
    s_yev = es.enter_context(nc.semaphore("s_yev"))
    s_ytr = es.enter_context(nc.semaphore("s_ytr"))
    s_ytv = es.enter_context(nc.semaphore("s_ytv"))
    s_ym = [es.enter_context(nc.semaphore(f"s_ym{j}")) for j in range(2)]
    s_yg = [es.enter_context(nc.semaphore(f"s_yg{j}")) for j in range(2)]
    s_cmb = es.enter_context(nc.semaphore("s_cmb"))
    s_wk = es.enter_context(nc.semaphore("s_wk"))
    s_akm = es.enter_context(nc.semaphore("s_akm"))
    s_akv = es.enter_context(nc.semaphore("s_akv"))
    s_avm = es.enter_context(nc.semaphore("s_avm"))
    s_avv = es.enter_context(nc.semaphore("s_avv"))
    s_akp = es.enter_context(nc.semaphore("s_akp"))
    s_akx = es.enter_context(nc.semaphore("s_akx"))
    s_akf = es.enter_context(nc.semaphore("s_akf"))
    s_akk = es.enter_context(nc.semaphore("s_akk"))
    s_akvd = es.enter_context(nc.semaphore("s_akvd"))
    s_kva = es.enter_context(nc.semaphore("s_kva"))
    s_kvin = es.enter_context(nc.semaphore("s_kvin"))
    s_kvl = es.enter_context(nc.semaphore("s_kvl"))
    s_wq2 = es.enter_context(nc.semaphore("s_wq2"))
    s_kvb = es.enter_context(nc.semaphore("s_kvb"))
    s_aw = es.enter_context(nc.semaphore("s_aw"))
    s_o1 = [es.enter_context(nc.semaphore(f"s_o1{j}")) for j in range(2)]
    s_moA = es.enter_context(nc.semaphore("s_moA"))
    s_awq = [es.enter_context(nc.semaphore(f"s_awq{j}")) for j in range(2)]
    s_aw2 = [es.enter_context(nc.semaphore(f"s_aw2{j}")) for j in range(2)]
    s_l3 = s_l1; B_l3 = 256
    s_atr = s_tr; B_atr = 64
    s_aev = s_evf; B_aev = 64
    s_axn = s_xn; B_axn = 32
    s_aqm = s_lgm; B_aqm = 32
    s_aqv = s_lgv; B_aqv = 32
    s_asq = s_sq; B_asq = 32
    s_ars = s_rst; B_ars = 1
    s_ast = s_bst; B_ast = 1
    s_asm = s_mt; B_asm = 3
    s_arw = s_fp; B_arw = 1
    s_a4t = s_tot; B_a4t = 1
    s_aqf = s_akp; B_aqf = 512
    s_aqx = s_akx; B_aqx = 512
    s_aqe = s_akf; B_aqe = 512
    s_azm = s_akm; B_azm = 64
    s_azv = s_akv; B_azv = 64
    s_azb = s_avm; B_azb = 64
    s_aqz = s_avv; B_aqz = 64
    s_aam = s_akk; B_aam = 1024
    s_aov = s_akvd; B_aov = 1
    s_axm = s_hm; B_axm = 1024
    s_axv = s_fv; B_axv = 1
    s_ax2 = s_lnv; B_ax2 = 64
    s_a2b = s_cst; B_a2b = 64
    s_ahm = s_ya; B_ahm = 64
    s_ahe = s_hev; B_ahe = 1024
    s_aw2c = s_trB; B_aw2c = 64
    s_axe = s_ytv; B_axe = 64
    s_ax4 = s_yev; B_ax4 = 64
    s_a4v = s_mid; B_a4v = 4
    s_awo = s_const_l = [s_const, s_const]; B_awo = 208
    s_aw1 = [s_kvin, s_kvin]; B_aw1 = 16
    block = es.enter_context(nc.Block())
    KSB0 = 16384
    KFB0 = KSB0 + 4096
    V650 = KFB0 + 512
    EXP0 = V650 + 4160
    PRJ0 = EXP0 + 512
    H2A0, W1B0, W2B0, WQB0, PRJ2, WOB0, QX0 = 0, 16384, 18432, 20480, 22528, 26624, 28672
    X1C, X2C, X2BC, AOC, QSC, QFC, OUTC = 0, 8192, 16384, 20480, 24576, 25600, 26624
    with es:
        NCONST = 18
        lv = {"A": 0, "B": 1, "C1": 2, "C2": 3, "C": 9}[stage]
        CC_YAG = 4 if lv >= 2 else 3
        stt = m1_a

        @block.sync
        def _(sync):
            for dst, src in [
                (ident[:], ident_i[:, :]), (lstrict[:], lstrict_i[:, :]),
                (iota8[:], iota8_i[:, :]), (ones1[:], ones1_i[:, :]),
                (onesc[:], onesc_i[:, :]), (basemat[:], basemat_i[:, :]),
                (jall[:], jall_i[:, :]), (gidxc[:], gidx_i[:, :]),
                (epsc[:], eps_i[:, :]), (wg_sb[:], wg[:, :]),
                (eb1_sb[:], eb1_i[:, :]), (eb2_sb[:], eb2_i[:, :]),
                (identb[:], identb_i[:, :]), (bout_s[:], bout_c[:, :]),
                (gam_s[:], gam_c[:, :]), (bet_s[:], bet_c[:, :]),
                (b1a_s[:], b1a_c[:, :]), (b2a_s[:], b2a_c[:, :]),
            ]:
                sync.dma_start(out=dst, in_=src).then_inc(s_const, 16)
            for i in range(NT):
                if i >= 2:
                    sync.wait_ge(s_sq, i - 1)
                    sync.wait_ge(s_red, i - 1)
                sync.dma_start(
                    out=xt[:, (i % 2) * D:(i % 2 + 1) * D],
                    in_=xm[i * 128:(i + 1) * 128, :],
                ).then_inc(s_l1[i % 2], 16)
            for i in range(NT):
                if i == 0:
                    sync.wait_ge(s_red, NT)
                    sync.wait_ge(s_sq, NT)
                if i >= 2:
                    sync.wait_ge(s_xn, i - 1)
                sync.dma_start(
                    out=xt[:, (i % 2) * D:(i % 2 + 1) * D],
                    in_=xm[i * 128:(i + 1) * 128, :],
                ).then_inc(s_l2[i % 2], 16)
            if lv >= 2:
                sync.wait_ge(s_akvd, 1)
                sync.dma_start(out=kv_in[:, :], in_=h_fm[:, 0:4160].bitcast(F32)
                               ).then_inc(s_kvin, 16)
            for st in range(NST):
                sync.wait_ge(s_ytv, st + 1)
                sync.dma_start(out=y_mine[st * 128:(st + 1) * 128, :],
                               in_=y_tm[:, 0:D]
                               ).then_inc(s_ym[st % 2], 16)

            for tg in range(8 if lv >= 9 else 0):
                for tt in range(4):
                    gt = tg * 4 + tt
                    if gt >= 2:
                        sync.wait_ge(s_axn, B_axn + gt - 1)
                    if tg == 0 and tt == 0:
                        sync.wait_ge(s_ya, 64)
                    sync.dma_start(
                        out=xt[:, (gt % 2) * D:(gt % 2 + 1) * D],
                        in_=xm[gt * 128:(gt + 1) * 128, :],
                    ).then_inc(s_l3[gt % 2], 16)
                if tg == 0:
                    sync.wait_ge(s_wk, 112)
                for qc in range(8):
                    u = tg * 8 + qc
                    if u >= 2:
                        sync.wait_ge(s_aqm, B_aqm + u - 1)
                    sync.dma_start(
                        out=ew2_sb[:, WQB0 + (u % 2) * 1024:WQB0 + (u % 2 + 1) * 1024],
                        in_=bass.AP(wq_bf.ap().tensor, wq_bf.ap().offset + qc * 128,
                                    [[8192, 128], [1024, 8], [1, 128]]),
                    ).then_inc(s_awq[u % 2], 16)
                for dc in range(8):
                    u = tg * 8 + dc
                    if u >= 2:
                        sync.wait_ge(s_axm, B_axm + u - 1)
                    sync.dma_start(
                        out=ew2_sb[:, WOB0 + (u % 2) * 1024:WOB0 + (u % 2 + 1) * 1024],
                        in_=wout_bf[:, dc * 1024:(dc + 1) * 1024],
                    ).then_inc(s_awo[0], 16)
                for hch in range(32):
                    u = tg * 32 + hch
                    if u >= 2:
                        sync.wait_ge(s_ahm, B_ahm + u - 1)
                    sync.dma_start(
                        out=ew2_sb[:, W1B0 + (u % 2) * 1024:W1B0 + (u % 2 + 1) * 1024],
                        in_=w1a_bf[:, hch * 1024:(hch + 1) * 1024],
                    ).then_inc(s_aw1[0], 16)
                for pss in range(2):
                    for hch in range(32):
                        u = tg * 64 + pss * 32 + hch
                        if u >= 2:
                            sync.wait_ge(s_aw2c, B_aw2c + u - 1)
                        sync.dma_start(
                            out=ew2_sb[:, W2B0 + (u % 2) * 1024:W2B0 + (u % 2 + 1) * 1024],
                            in_=w2a_bf[:, hch * 1024:(hch + 1) * 1024],
                        ).then_inc(s_aw2[u % 2], 16)
                for tt in range(4):
                    gt = tg * 4 + tt
                    sync.wait_ge(s_a4v, B_a4v + gt + 1)
                    sync.dma_start(
                        out=out_d[gt * 128:(gt + 1) * 128, :],
                        in_=ew1_sb[:, OUTC + (gt % 2) * 2048:OUTC + (gt % 2) * 2048 + 2048]
                        .bitcast(F32),
                    ).then_inc(s_o1[gt % 2], 16)

            sync.wait_ge(s_gr, 16 * NST)
            sync.wait_ge(s_fv, 1)
            sync.dma_start(out=dbg_rt[:, 0 * NT:1 * NT], in_=e0_a[:]).then_inc(s_out, 16)
            sync.dma_start(out=dbg_rt[:, 1 * NT:2 * NT], in_=e1_a[:]).then_inc(s_out, 16)
            sync.dma_start(out=dbg_rt[:, 2 * NT:3 * NT], in_=g0_a[:]).then_inc(s_out, 16)
            sync.dma_start(out=dbg_rt[:, 3 * NT:4 * NT], in_=g1_a[:]).then_inc(s_out, 16)
            sync.dma_start(out=dbg_rt[:, 4 * NT:5 * NT], in_=mu_a[:]).then_inc(s_out, 16)
            sync.dma_start(out=dbg_rt[:, 5 * NT:6 * NT], in_=rstd_a[:]).then_inc(s_out, 16)
            sync.dma_start(out=dbg_rt[:, 6 * NT:8 * NT], in_=pos_a[:]).then_inc(s_out, 16)
            sync.dma_start(out=dbg_rec[:, :], in_=myrec[:]).then_inc(s_out, 16)
            sync.wait_ge(s_out, 16 * 8)
            if lv >= 9:
                sync.wait_ge(s_moA, 16 * NT)

        @block.scalar
        def _(scalar):
            scalar.wait_ge(s_const, 16 * NCONST)
            for i in range(NT):
                p2 = i % 2
                xs = xt[:, (i % 2) * D:(i % 2 + 1) * D]
                scalar.wait_ge(s_l1[i % 2], 16 * (i // 2 + 1))
                if i >= 2:
                    scalar.wait_ge(s_red, i - 1)
                scalar.activation(xn_tm[:, p2 * D:(p2 + 1) * D], xs,
                                  AF.Square).then_inc(s_sq, 1)
            scalar.wait_ge(s_bst, 1)
            scalar.activation(var_a[:], var_a[:], AF.Ln, bias=epsc[:, 0:1])
            scalar.drain()
            scalar.activation(rstd_a[:], var_a[:], AF.Exp, scale=-0.5)
            scalar.drain().then_inc(s_rst, 1)
            for i in range(NT):
                p2 = i % 2
                scalar.wait_ge(s_evf, 2 * i + 2)
                for ch in range(NCH):
                    ins = scalar.copy(
                        xn_fm[:, ch * TPC + i * 128: ch * TPC + (i + 1) * 128],
                        xn_f32[:, p2 * D + ch * 128: p2 * D + (ch + 1) * 128],
                    )
                ins.then_inc(s_ebf, 1)
            scalar.wait_ge(s_t2, 1)
            scalar.activation(g0_a[:], d0_a[:], AF.Sigmoid)
            scalar.activation(g1_a[:], d1_a[:], AF.Sigmoid)
            scalar.drain().then_inc(s_sig, 1)

            for tg in range(8 if lv >= 2 else 0):
                for tt in range(4):
                    gtt = tg * 4 + tt
                    for h in range(16):
                        kp = gtt * 16 + h
                        scalar.wait_ge(s_akp, kp + 1)
                        if kp >= 2:
                            scalar.wait_ge(s_akf, kp - 1)
                        scalar.activation(
                            ew2_sb[:, EXP0 + (kp % 2) * 256:EXP0 + (kp % 2 + 1) * 256],
                            ps_lg[:, (h % 2) * 256:(h % 2) * 256 + 256], AF.Exp)
                        scalar.drain().then_inc(s_akx, 1)

            for st in range(NST):
                scalar.wait_ge(s_lnv, st + 1)
                if st >= 2:
                    scalar.wait_ge(s_trB, st - 1)
                scalar.activation(xgb[:, (st % 2) * D:(st % 2 + 1) * D],
                                  xgl[:, (st % 2) * D:(st % 2 + 1) * D],
                                  AF.Copy)
                scalar.drain().then_inc(s_cst, 1)
                if st % 2 == 1:
                    g = st // 2
                    if g == 0 and lv >= 2:
                        scalar.wait_ge(s_kvin, 16)
                    for hch in range(32):
                        scalar.wait_ge(s_hm, g * 32 + hch + 1)
                        src = ps_lg if hch % 2 == 0 else ps_pos
                        scalar.activation(h_fm[:, hch * 256:(hch + 1) * 256],
                                          src[:, 0:256],
                                          AF.Gelu, bias=eb1_sb[:, hch:hch + 1])
                        scalar.drain().then_inc(s_hev, 1)


            if lv < 9:
                return
            scalar.wait_ge(s_ya, 64)
            XSs = ew1_sb[:, 0:8192].bitcast(F32)
            SQs = ew1_sb[:, 8192:16384].bitcast(F32)
            for tg in range(8):
                for qc in range(8):
                    for hq in range(2):
                        hh = tg * 16 + qc * 2 + hq
                        for fh in range(2):
                            c_qf = hh * 2 + fh
                            scalar.wait_ge(s_aqf, B_aqf + c_qf + 1)
                            if c_qf >= 2:
                                scalar.wait_ge(s_aqe, B_aqe + c_qf - 1)
                            scalar.activation(
                                ew2_sb[:, QX0 + (c_qf % 2) * 512:QX0 + (c_qf % 2 + 1) * 512],
                                ps_m3[:, 0:512], AF.Exp)
                            scalar.drain().then_inc(s_aqx, 1)
                for dc in range(8):
                    scalar.wait_ge(s_axv, B_axv + tg * 8 + dc + 1)
                    scalar.activation(
                        bass.AP(SQs.tensor, SQs.offset + dc * 512,
                                [[SQs.ap[0][0], 128], [1, 512]]),
                        bass.AP(XSs.tensor, XSs.offset + dc * 512,
                                [[XSs.ap[0][0], 128], [1, 512]]),
                        AF.Square)
                    scalar.drain().then_inc(s_asq, 1)
                scalar.wait_ge(s_ast, B_ast + (tg * 2) * 2 + 1)
                scalar.activation(stt[:, 8:12], stt[:, 4:8], AF.Ln, bias=epsc[:, 0:1])
                scalar.drain()
                scalar.activation(stt[:, 8:12], stt[:, 8:12], AF.Exp, scale=-0.5)
                scalar.drain().then_inc(s_ars, 1)
                for dc in range(8):
                    scalar.wait_ge(s_ax2, B_ax2 + tg + 1)
                    scalar.copy(
                        ew1_sb[:, X2BC + dc * 512:X2BC + (dc + 1) * 512],
                        bass.AP(SQs.tensor, SQs.offset + dc * 512,
                                [[SQs.ap[0][0], 128], [1, 512]]))
                    scalar.drain().then_inc(s_a2b, 1)
                for hch in range(32):
                    uu = tg * 32 + hch
                    scalar.wait_ge(s_ahm, B_ahm + uu + 1)
                    srch = ps_lg if hch % 2 == 0 else ps_pos
                    scalar.activation(ew2_sb[:, H2A0 + hch * 512:H2A0 + (hch + 1) * 512],
                                      srch[:, 0:512], AF.Gelu,
                                      bias=b1a_s[:, hch:hch + 1])
                    scalar.drain().then_inc(s_ahe, 1)
                for dc in range(8):
                    scalar.wait_ge(s_axe, B_axe + tg * 2 + 2)
                    scalar.activation(
                        bass.AP(SQs.tensor, SQs.offset + dc * 512,
                                [[SQs.ap[0][0], 128], [1, 512]]),
                        bass.AP(XSs.tensor, XSs.offset + dc * 512,
                                [[XSs.ap[0][0], 128], [1, 512]]),
                        AF.Square)
                    scalar.drain().then_inc(s_asq, 1)
                scalar.wait_ge(s_ast, B_ast + (tg * 2 + 1) * 2 + 1)
                scalar.activation(stt[:, 8:12], stt[:, 4:8], AF.Ln, bias=epsc[:, 0:1])
                scalar.drain()
                scalar.activation(stt[:, 8:12], stt[:, 8:12], AF.Exp, scale=-0.5)
                scalar.drain().then_inc(s_ars, 1)

        @block.vector
        def _(vector):
            vector.wait_ge(s_const, 16 * NCONST)
            for i in range(NT):
                p2 = i % 2
                xs = xt[:, (i % 2) * D:(i % 2 + 1) * D]
                vector.wait_ge(s_l1[i % 2], 16 * (i // 2 + 1))
                vector.tensor_reduce(sums[:, i:i + 1], xs, AX.X, ALU.add)
                vector.wait_ge(s_sq, i + 1)
                vector.tensor_reduce(ssqs[:, i:i + 1],
                                     xn_tm[:, p2 * D:(p2 + 1) * D], AX.X,
                                     ALU.add).then_inc(s_red, 1)
            vector.drain()
            vector.tensor_scalar(mu_a[:], sums[:], 1.0 / D, None, ALU.mult)
            vector.tensor_scalar(var_a[:], ssqs[:], 1.0 / D, None, ALU.mult)
            vector.drain()
            vector.tensor_tensor(fns_a[:, 0:NT], mu_a[:], mu_a[:], ALU.mult)
            vector.drain()
            vector.tensor_tensor(var_a[:], var_a[:], fns_a[:, 0:NT], ALU.subtract)
            vector.drain().then_inc(s_bst, 1)
            vector.wait_ge(s_rst, 1)
            for i in range(NT):
                p2 = i % 2
                xs = xt[:, (i % 2) * D:(i % 2 + 1) * D]
                vector.wait_ge(s_l2[i % 2], 16 * (i // 2 + 1))
                xnd = xn_tm[:, p2 * D:(p2 + 1) * D]
                if i >= 2:
                    vector.wait_ge(s_tr, 2 * (i - 1))
                vector.tensor_scalar(xnd, xs, mu_a[:, i:i + 1], rstd_a[:, i:i + 1],
                                     ALU.subtract, ALU.mult)
                vector.drain().then_inc(s_xn, 1)
                if i >= 2:
                    vector.wait_ge(s_lgm, i - 1)
                    vector.wait_ge(s_ebf, i - 1)
                for h2 in range(2):
                    vector.wait_ge(s_tr, 2 * i + h2 + 1)
                    src = ps_tr if h2 == 0 else ps_tr2
                    vector.tensor_copy(
                        xn_f32[:, p2 * D + h2 * 512: p2 * D + (h2 + 1) * 512],
                        src[:])
                    vector.drain().then_inc(s_evf, 1)
                vector.wait_ge(s_lgm, i + 1)
                vector.tensor_copy(lg_all[:, i * 8:(i + 1) * 8],
                                   ps_lg[:, (i % 2) * 16:(i % 2) * 16 + 8])
                vector.drain().then_inc(s_lgv, 1)

            LG = view3(lg_all, 128, NT, 8)
            io_ap = iota8.ap()
            IOB = bass.AP(io_ap.tensor, io_ap.offset,
                          [list(io_ap.ap[0]), [0, NT], [1, 8]])
            W1 = view3(fns_a, 128, NT, 8)
            W2 = view3(fns_a, 128, NT, 8, offset=NT * 8)
            OH0 = view3(oh_all, 128, NT, 8, offset=0, gstride=16)
            OH1 = view3(oh_all, 128, NT, 8, offset=8, gstride=16)
            vector.tensor_reduce(m1_a[:], LG, AX.X, ALU.max)
            vector.drain()
            vector.tensor_tensor(W1, LG, bcast3(m1_a, 128, NT, 8), ALU.is_equal)
            vector.tensor_tensor(W2, W1, IOB, ALU.mult)
            vector.tensor_scalar(W1, W1, -OOB, OOB, ALU.mult, ALU.add)
            vector.tensor_tensor(W2, W2, W1, ALU.add)
            vector.tensor_reduce(e0_a[:], W2, AX.X, ALU.min)
            vector.drain()
            vector.tensor_tensor(OH0, IOB, bcast3(e0_a, 128, NT, 8), ALU.is_equal)
            vector.drain()
            vector.tensor_tensor(W1, OH0, bcast3(m1_a, 128, NT, 8), ALU.bypass) if False else None
            vector.tensor_scalar(W1, OH0, OOB, None, ALU.mult)
            vector.tensor_tensor(W2, LG, W1, ALU.subtract)
            vector.tensor_reduce(m2_a[:], W2, AX.X, ALU.max)
            vector.drain()
            vector.tensor_tensor(W1, W2, bcast3(m2_a, 128, NT, 8), ALU.is_equal)
            vector.tensor_tensor(W2, W1, IOB, ALU.mult)
            vector.tensor_scalar(W1, W1, -OOB, OOB, ALU.mult, ALU.add)
            vector.tensor_tensor(W2, W2, W1, ALU.add)
            vector.tensor_reduce(e1_a[:], W2, AX.X, ALU.min)
            vector.drain()
            vector.tensor_tensor(OH1, IOB, bcast3(e1_a, 128, NT, 8), ALU.is_equal)
            vector.tensor_tensor(d0_a[:], m1_a[:], m2_a[:], ALU.subtract)
            vector.tensor_tensor(d1_a[:], m2_a[:], m1_a[:], ALU.subtract)
            vector.drain().then_inc(s_t2, 1)

            vector.wait_ge(s_tot, 1)
            vector.tensor_copy(tot_sb[:], ps_lg[0:16, 32:64])
            vector.drain()
            vector.tensor_reduce(cnt_sb[:], tot_sb[:], AX.X, ALU.add)
            vector.drain().then_inc(s_mid, 1)
            vector.wait_ge(s_mt, 1)
            vector.tensor_copy(basef[:], ps_lg[0:1, 64:80])
            vector.drain().then_inc(s_mid, 1)
            vector.wait_ge(s_mt, 2)
            vector.tensor_copy(totT_sb[:], ps_pos[0:NT, 96:112])
            vector.drain().then_inc(s_mid, 1)
            vector.wait_ge(s_mt, 3)
            vector.tensor_copy(runp_sb[:], ps_pos[0:NT, 64:80])
            vector.drain().then_inc(s_mid, 1)

            vector.wait_ge(s_fp, 1)
            psp_ap = ps_pos.ap()
            PSP = bass.AP(psp_ap.tensor, psp_ap.offset,
                          [list(psp_ap.ap[0]), [8, NT * 2], [1, 8]])
            FN = view3(fns_a, 128, NT * 2, 8)
            OHB = view3(oh_all, 128, NT * 2, 8)
            vector.tensor_tensor(FN, OHB, PSP, ALU.mult)
            vector.tensor_reduce(pos_a[:], FN, AX.X, ALU.add)
            vector.drain()
            vector.tensor_scalar(keep_a[:], pos_a[:], float(CAP), None, ALU.is_lt)
            vector.drain()
            G0V = view3(g0_a, 128, NT, 1)
            G1V = view3(g1_a, 128, NT, 1)
            K0 = view3(keep_a, 128, NT, 1, offset=0, gstride=2)
            K1 = view3(keep_a, 128, NT, 1, offset=1, gstride=2)
            vector.wait_ge(s_sig, 1)
            vector.tensor_tensor(G0V, G0V, K0, ALU.mult)
            vector.tensor_tensor(G1V, G1V, K1, ALU.mult)
            S0 = view3(sidx_f, 128, NT, 1, offset=0, gstride=2)
            S1 = view3(sidx_f, 128, NT, 1, offset=1, gstride=2)
            vector.tensor_scalar(S0, view3(e0_a, 128, NT, 1), float(CAP), None, ALU.mult)
            vector.tensor_scalar(S1, view3(e1_a, 128, NT, 1), float(CAP), None, ALU.mult)
            vector.drain()
            vector.tensor_tensor(sidx_f[:], sidx_f[:], pos_a[:], ALU.add)
            vector.tensor_scalar(fns_a[:, 0:NT * 2], keep_a[:], -OOB, OOB,
                                 ALU.mult, ALU.add)
            vector.drain()
            vector.tensor_tensor(sidx_f[:], sidx_f[:], fns_a[:, 0:NT * 2], ALU.add)
            vector.drain()
            vector.tensor_copy(sidx_a[:], sidx_f[:])
            rj_ap = lg_all.ap()
            RJ = bass.AP(rj_ap.tensor, rj_ap.offset,
                         [list(rj_ap.ap[0]), [4, NT * 2], [1, 1]])
            vector.tensor_copy(RJ, view3(jall, 128, NT * 2, 1))
            RM = bass.AP(rj_ap.tensor, rj_ap.offset + 1,
                         [list(rj_ap.ap[0]), [8, NT], [4, 2], [1, 1]])
            vector.tensor_copy(RM, bcast_mid(mu_a, 128, NT, 2, 1))
            RS = bass.AP(rj_ap.tensor, rj_ap.offset + 2,
                         [list(rj_ap.ap[0]), [8, NT], [4, 2], [1, 1]])
            vector.tensor_copy(RS, bcast_mid(rstd_a, 128, NT, 2, 1))
            RZ = bass.AP(rj_ap.tensor, rj_ap.offset + 3,
                         [list(rj_ap.ap[0]), [4, NT * 2], [1, 1]])
            vector.memset(RZ, 0.0)
            vector.drain()
            vector.tensor_scalar(tidx_f[:, 0:NT * 2], keep_a[:], -OOB, OOB,
                                 ALU.mult, ALU.add)
            vector.drain()
            vector.tensor_tensor(tidx_f[:, 0:NT * 2], sidx_f[:],
                                 tidx_f[:, 0:NT * 2], ALU.subtract)
            vector.drain()
            vector.tensor_tensor(tidx_f[:, 0:NT * 2], tidx_f[:, 0:NT * 2],
                                 keep_a[:], ALU.mult)
            vector.drain()
            vector.tensor_copy(yidx_a[:], tidx_f[:, 0:NT * 2])
            vector.drain().then_inc(s_fv, 1)

            vector.wait_ge(s_gr, 16 * NST)
            MRJ = bass.AP(myrec.ap().tensor, myrec.ap().offset,
                          [list(myrec.ap().ap[0]), [4, NST], [1, 1]])
            MRR = bass.AP(myrec.ap().tensor, myrec.ap().offset + 2,
                          [list(myrec.ap().ap[0]), [4, NST], [1, 1]])
            TF0 = view3(tidx_f, 128, NST, 1)
            TF1 = view3(tidx_f, 128, NST, 1, offset=NST)
            vector.tensor_scalar(TF1, MRR, 0.0, None, ALU.is_gt)
            vector.drain()
            vector.tensor_scalar(tidx_f[:, NST:2 * NST], tidx_f[:, NST:2 * NST],
                                 -float(1 << 27), float(1 << 27), ALU.mult, ALU.add)
            vector.drain()
            vector.tensor_tensor(TF0, MRJ, TF1, ALU.add)
            vector.drain()
            vector.tensor_copy(tidx_a[:], tidx_f[:, 0:NST])
            vector.drain()
            vector.tensor_scalar(tidx_a[:], tidx_a[:], 1, None,
                                 ALU.arith_shift_right)
            vector.drain().then_inc(s_tix, 1)

            KVB = [ps_m0, ps_m1, ps_m2, ps_m3, ps_pos]
            KVACC = h_fm[:, 0:4160].bitcast(F32)
            if lv >= 2:
                vector.memset(ew2_sb[:, V650:V650 + 4160], 1.0)
                vector.memset(KVACC[:, 0:2080], 0.0)
                vector.drain()
            for tg in range(8 if lv >= 2 else 0):
                for oc in range(8):
                    ktg = tg * 8 + oc
                    vector.wait_ge(s_akm, ktg + 1)
                    if tg >= 1:
                        vector.wait_ge(s_akp, tg * 64)
                    vector.tensor_copy(
                        ew2_sb[:, KSB0 + oc * 512:KSB0 + (oc + 1) * 512],
                        ps_tr[:, 0:512])
                    vector.drain().then_inc(s_akv, 1)
                for tt in range(4):
                    gtt = tg * 4 + tt
                    for vh in range(2):
                        idx_v = gtt * 2 + vh
                        vector.wait_ge(s_avm, idx_v + 1)
                        if gtt >= 4:
                            vector.wait_ge(s_akk, 32 * (gtt - 3))
                        dst = bass.AP(ew2_sb.ap().tensor,
                                      ew2_sb.ap().offset + V650 + (gtt % 4) * 1040 + vh * 8 * 65,
                                      [[ew2_sb.ap().ap[0][0], 128], [65, 8], [1, 64]])
                        src = bass.AP(ps_tr2.ap().tensor, ps_tr2.ap().offset,
                                      [[512, 128], [64, 8], [1, 64]])
                        vector.tensor_copy(dst, src)
                        vector.drain().then_inc(s_avv, 1)
                for tt in range(4):
                    gtt = tg * 4 + tt
                    for h in range(16):
                        kp = gtt * 16 + h
                        vector.wait_ge(s_akx, kp + 1)
                        if kp >= 2:
                            vector.wait_ge(s_akk, 2 * (kp - 1))
                        vector.tensor_scalar(fns_a[:, 0:256],
                                             ps_lg[:, (h % 2) * 256:(h % 2) * 256 + 256],
                                             0.0, None, ALU.max)
                        vector.tensor_scalar(fns_a[:, 256:512],
                                             ew2_sb[:, EXP0 + (kp % 2) * 256:
                                                    EXP0 + (kp % 2 + 1) * 256],
                                             1.0, None, ALU.min)
                        vector.drain()
                        vector.tensor_tensor(
                            ew2_sb[:, KFB0 + (kp % 2) * 256:KFB0 + (kp % 2 + 1) * 256],
                            fns_a[:, 256:512], fns_a[:, 0:256], ALU.add)
                        vector.drain().then_inc(s_akf, 1)
                    vector.wait_ge(s_akk, 2 * 16 * (gtt + 1))
                    for b in range(4):
                        vector.tensor_tensor(KVACC[:, b * 512:(b + 1) * 512],
                                             KVACC[:, b * 512:(b + 1) * 512],
                                             KVB[b][:, 0:512], ALU.add)
                    vector.tensor_tensor(KVACC[:, 2048:2080], KVACC[:, 2048:2080],
                                         ps_pos[:, 0:32], ALU.add)
                    vector.drain().then_inc(s_kva, 1)
            if lv >= 2:
                vector.wait_ge(s_kva, NT)
                vector.drain().then_inc(s_akvd, 1)

            for st in range(NST):
                g = st // 2
                vector.wait_ge(s_xg[st % 2], 16 * (st // 2 + 1))
                mu_c = myrec[:, st * 4 + 1:st * 4 + 2]
                rs_c = myrec[:, st * 4 + 2:st * 4 + 3]
                vector.tensor_scalar(xgl[:, (st % 2) * D:(st % 2 + 1) * D],
                                     xgl[:, (st % 2) * D:(st % 2 + 1) * D],
                                     mu_c, rs_c, ALU.subtract, ALU.mult)
                vector.drain().then_inc(s_lnv, 1)
                vector.wait_ge(s_trB, st + 1)
                if g >= 2:
                    vector.wait_ge(s_hm, (g - 1) * 32)
                pb = ps_tr[:].bitcast(BF16)
                vector.tensor_copy(
                    bass.AP(disp_fm.ap().tensor,
                            disp_fm.ap().offset + (g % 2) * 2048 + (st % 2) * 128,
                            [[disp_fm.ap().ap[0][0], 128], [256, NCH], [1, 128]]),
                    bass.AP(pb.tensor, pb.offset,
                            [[pb.ap[0][0], 128], [128, NCH], [1, 128]]))
                vector.drain().then_inc(s_evB, 1)
                if st % 2 == 1:
                    g2 = st // 2
                    for pss in range(2):
                        vector.wait_ge(s_ya, g2 * 2 + pss + 1)
                        if g2 >= 1 and pss == 0:
                            vector.wait_ge(s_ytr, 2 * g2)
                        for dq in range(4):
                            dch = pss * 4 + dq
                            src = [ps_m0, ps_m1, ps_m2, ps_m3][dq]
                            vector.tensor_scalar(y_fm[:, dch * 256:(dch + 1) * 256],
                                                 src[:, 0:256],
                                                 eb2_sb[:, dch:dch + 1], None, ALU.add)
                        vector.drain().then_inc(s_yev, 1)
                    for q in range(2):
                        st2 = g2 * 2 + q
                        vector.wait_ge(s_ytr, st2 + 1)
                        if st2 >= 1:
                            vector.wait_ge(s_ym[(st2 - 1) % 2], 16 * ((st2 - 1) // 2 + 1))
                        pb2 = ps_tr2[:].bitcast(BF16)
                        vector.tensor_copy(
                            y_tm[:, 0:D],
                            bass.AP(pb2.tensor, pb2.offset,
                                    [[pb2.ap[0][0], 128], [128, NCH], [1, 128]]))
                        vector.drain().then_inc(s_ytv, 1)

            if lv >= 9:
                vector.wait_ge(s_kvl, 16)
                vector.tensor_copy(h_fm[:, 4160:6240], h_fm[:, 0:4160].bitcast(F32))
                vector.drain().then_inc(s_kvb, 1)
                vector.wait_ge(s_hm, 32 * 32)
                XS = ew1_sb[:, 0:8192].bitcast(F32)
                SQ = ew1_sb[:, 8192:16384].bitcast(F32)
                MUB = ew1_sb[:, AOC:AOC + 1024].bitcast(F32)
                RSB = ew1_sb[:, AOC + 1024:AOC + 2048].bitcast(F32)
                SCR2 = ew1_sb[:, 30720:31744].bitcast(F32)
                for tg in range(8):
                    tgp = tg % 2
                    for tt in range(4):
                        gt = tg * 4 + tt
                        vector.wait_ge(s_l3[gt % 2], B_l3 + 16 * (gt // 2 + 1))
                        xres = (xgl[:, tt * D:(tt + 1) * D] if tt < 2
                                else xn_tm[:, (tt - 2) * D:(tt - 1) * D])
                        xsl = xt[:, (gt % 2) * D:(gt % 2 + 1) * D]
                        vector.tensor_copy(xres, xsl)
                        vector.tensor_scalar(xn_f32[:, (gt % 2) * D:(gt % 2 + 1) * D],
                                             xsl, mu_a[:, gt:gt + 1], rstd_a[:, gt:gt + 1],
                                             ALU.subtract, ALU.mult)
                        vector.drain().then_inc(s_axn, 1)
                        for h2 in range(2):
                            vector.wait_ge(s_atr, B_atr + 2 * gt + h2 + 1)
                            if tg >= 1 and tt == 0 and h2 == 0:
                                vector.wait_ge(s_axv, B_axv + 8 * tg)
                            src = ps_tr if h2 == 0 else ps_tr2
                            dstv = bass.AP(disp_fm.ap().tensor,
                                           disp_fm.ap().offset + h2 * 4 * 512 + tt * 128,
                                           [[disp_fm.ap().ap[0][0], 128], [512, 4], [1, 128]])
                            srcv = bass.AP(src.ap().tensor, src.ap().offset,
                                           [[512, 128], [128, 4], [1, 128]])
                            vector.tensor_copy(dstv, srcv)
                            vector.drain().then_inc(s_aev, 1)
                    for qc in range(8):
                        u = tg * 8 + qc
                        vector.wait_ge(s_aqm, B_aqm + u + 1)
                        vector.tensor_copy(ew1_sb[:, QSC + (u % 2) * 512:QSC + (u % 2 + 1) * 512],
                                           ps_m2[:, 0:512])
                        vector.drain().then_inc(s_aqv, 1)
                        for hq in range(2):
                            h = qc * 2 + hq
                            hh = tg * 16 + h
                            for fh in range(2):
                                c_qf = hh * 2 + fh
                                vector.wait_ge(s_aqx, B_aqx + c_qf + 1)
                                vector.tensor_scalar(fns_a[:, 0:512], ps_m3[:, 0:512],
                                                     0.0, None, ALU.max)
                                vector.tensor_scalar(SCR2[:, 0:512],
                                                     ew2_sb[:, QX0 + (c_qf % 2) * 512:
                                                            QX0 + (c_qf % 2 + 1) * 512],
                                                     1.0, None, ALU.min)
                                vector.drain()
                                vector.tensor_tensor(
                                    ew1_sb[:, QFC + fh * 512:QFC + (fh + 1) * 512],
                                    SCR2[:, 0:512], fns_a[:, 0:512], ALU.add)
                                vector.drain().then_inc(s_aqe, 1)
                            vector.wait_ge(s_azm, B_azm + hh + 1)
                            zrow = y_tm[:].bitcast(F32)[0:1, 0:512]
                            vector.tensor_scalar(zrow, ps_lg[0:1, 0:512], EPS_ATTN,
                                                 None, ALU.add)
                            vector.drain()
                            vector.reciprocal(zrow, zrow)
                            vector.drain().then_inc(s_azv, 1)
                            vector.wait_ge(s_azb, B_azb + hh + 1)
                            for fh in range(2):
                                vector.tensor_tensor(
                                    ew1_sb[:, QFC + fh * 512:QFC + (fh + 1) * 512],
                                    ew1_sb[:, QFC + fh * 512:QFC + (fh + 1) * 512],
                                    ps_lg[:, 0:512], ALU.mult)
                            vector.drain().then_inc(s_aqz, 1)
                        vector.wait_ge(s_aam, B_aam + tg * 16 + qc * 2 + 2)
                        bank = ps_m0 if qc % 2 == 0 else ps_m1
                        vector.tensor_copy(ew1_sb[:, AOC + qc * 512:AOC + (qc + 1) * 512],
                                           bank[:, 0:512])
                        vector.drain().then_inc(s_aov, 1)
                    for dc in range(8):
                        u = tg * 8 + dc
                        vector.wait_ge(s_axm, B_axm + u + 1)
                        bank = ps_m2 if dc % 2 == 0 else ps_m3
                        vector.scalar_tensor_tensor(
                            bass.AP(XS.tensor, XS.offset + dc * 512,
                                    [[XS.ap[0][0], 128], [1, 512]]),
                            bank[:, 0:512], bout_s[:, dc:dc + 1],
                            disp_fm[:, dc * 512:(dc + 1) * 512],
                            ALU.add, ALU.add)
                        vector.drain().then_inc(s_axv, 1)
                    for r in range(2):
                        lr = tg * 2 + r
                        vector.wait_ge(s_asm, B_asm + lr + 1)
                        vector.tensor_copy(stt[:, 0:8], ps_lg[:, 256:264])
                        vector.drain()
                        vector.tensor_scalar(stt[:, 0:4], stt[:, 0:4], 1.0 / D, None, ALU.mult)
                        vector.tensor_scalar(stt[:, 4:8], stt[:, 4:8], 1.0 / D, None, ALU.mult)
                        vector.drain()
                        vector.tensor_tensor(stt[:, 12:16], stt[:, 0:4], stt[:, 0:4], ALU.mult)
                        vector.drain()
                        vector.tensor_tensor(stt[:, 4:8], stt[:, 4:8], stt[:, 12:16],
                                             ALU.subtract)
                        vector.drain().then_inc(s_ast, 1)
                        vector.wait_ge(s_arw, B_arw + 2 * lr + 1)
                        vector.tensor_copy(xn_f32[0:1, 0:512], ps_pos[0:1, 0:512])
                        vector.tensor_copy(xn_f32[0:1, 512:1024], ps_m2[0:1, 0:512])
                        vector.drain().then_inc(s_ast, 1)
                        vector.wait_ge(s_arw, B_arw + 2 * lr + 2)
                        vector.tensor_copy(MUB[:, 0:512], ps_pos[:, 0:512])
                        vector.tensor_copy(RSB[:, 0:512], ps_m2[:, 0:512])
                        vector.drain()
                        for dc in range(8):
                            vector.tensor_tensor(
                                bass.AP(SQ.tensor, SQ.offset + dc * 512,
                                        [[SQ.ap[0][0], 128], [1, 512]]),
                                bass.AP(XS.tensor, XS.offset + dc * 512,
                                        [[XS.ap[0][0], 128], [1, 512]]),
                                MUB[:, 0:512], ALU.subtract)
                        vector.drain()
                        for dc in range(8):
                            vector.tensor_tensor(
                                bass.AP(SQ.tensor, SQ.offset + dc * 512,
                                        [[SQ.ap[0][0], 128], [1, 512]]),
                                bass.AP(SQ.tensor, SQ.offset + dc * 512,
                                        [[SQ.ap[0][0], 128], [1, 512]]),
                                RSB[:, 0:512], ALU.mult)
                        vector.drain()
                        for dc in range(8):
                            vector.tensor_scalar(
                                bass.AP(SQ.tensor, SQ.offset + dc * 512,
                                        [[SQ.ap[0][0], 128], [1, 512]]),
                                bass.AP(SQ.tensor, SQ.offset + dc * 512,
                                        [[SQ.ap[0][0], 128], [1, 512]]),
                                gam_s[:, dc:dc + 1], bet_s[:, dc:dc + 1],
                                ALU.mult, ALU.add)
                        if r == 0:
                            vector.drain().then_inc(s_ax2, 1)
                            for pss in range(2):
                                vector.wait_ge(s_aw2c, B_aw2c + tg * 64 + pss * 32 + 32)
                                for dq in range(4):
                                    dch = pss * 4 + dq
                                    dstb = [ps_m0, ps_m1, ps_m2, ps_m3][dq]
                                    vector.scalar_tensor_tensor(
                                        bass.AP(XS.tensor, XS.offset + dch * 512,
                                                [[XS.ap[0][0], 128], [1, 512]]),
                                        dstb[:, 0:512], b2a_s[:, dch:dch + 1],
                                        bass.AP(SQ.tensor, SQ.offset + dch * 512,
                                                [[SQ.ap[0][0], 128], [1, 512]]),
                                        ALU.add, ALU.add)
                                vector.drain().then_inc(s_axe, 1)
                        else:
                            vector.drain().then_inc(s_ax4, 1)
                    for tt in range(4):
                        gt = tg * 4 + tt
                        vector.wait_ge(s_a4t, B_a4t + 2 * gt + 2)
                        if gt >= 2:
                            vector.wait_ge(s_o1[gt % 2], 16 * ((gt - 2) // 2 + 1))
                        xres = (xgl[:, tt * D:(tt + 1) * D] if tt < 2
                                else xn_tm[:, (tt - 2) * D:(tt - 1) * D])
                        OUT = ew1_sb[:, OUTC + (gt % 2) * 2048:OUTC + (gt % 2) * 2048 + 2048].bitcast(F32)
                        for h2 in range(2):
                            src = ps_tr if h2 == 0 else ps_tr2
                            vector.tensor_tensor(OUT[:, h2 * 512:(h2 + 1) * 512],
                                                 xres[:, h2 * 512:(h2 + 1) * 512],
                                                 src[:], ALU.add)
                        vector.drain().then_inc(s_a4v, 1)

            vector.wait_ge(s_cc, CC_YAG)
            for i in range(NT):
                vector.wait_ge(s_yg[i % 2], 32 * (i // 2 + 1))
                if i >= 1 and lv >= 9:
                    vector.wait_ge(s_moA, 16 * i)
                mt = xgl[:, (i % 2) * D:(i % 2 + 1) * D]
                vector.tensor_scalar(mt, y_fm[:, (i % 2) * D:(i % 2 + 1) * D],
                                     g0_a[:, i:i + 1], None, ALU.mult)
                vector.drain()
                vector.scalar_tensor_tensor(y_tm[:, 0:D],
                                            xgb[:, (i % 2) * D:(i % 2 + 1) * D],
                                            g1_a[:, i:i + 1], mt, ALU.mult, ALU.add)
                vector.drain().then_inc(s_cmb, 1)

        @block.tensor
        def _(tensor):
            tensor.wait_ge(s_const, 16 * NCONST)
            for i in range(NT):
                p2 = i % 2
                tensor.wait_ge(s_xn, i + 1)
                if i >= 1:
                    tensor.wait_ge(s_evf, 2 * i)
                for h2 in range(2):
                    dst = ps_tr if h2 == 0 else ps_tr2
                    for q in range(4):
                        ch = h2 * 4 + q
                        ins = tensor.transpose(
                            dst[:, q * 128:(q + 1) * 128],
                            xn_tm[:, p2 * D + ch * 128: p2 * D + (ch + 1) * 128],
                            ident[:])
                    ins.then_inc(s_tr, 1)
                tensor.wait_ge(s_evf, 2 * i + 2)
                if i >= 2:
                    tensor.wait_ge(s_lgv, i - 1)
                for ch in range(NCH):
                    ins = tensor.matmul(
                        ps_lg[:, (i % 2) * 16:(i % 2) * 16 + 8],
                        xn_f32[:, p2 * D + ch * 128: p2 * D + (ch + 1) * 128],
                        wg_sb[:, ch * 8:(ch + 1) * 8],
                        start=(ch == 0), stop=(ch == NCH - 1))
                ins.then_inc(s_lgm, 1)
            tensor.wait_ge(s_t2, 1)
            for i in range(NT):
                ins = tensor.matmul(ps_lg[0:16, 32 + i:33 + i],
                                    oh_all[:, i * 16:(i + 1) * 16], onesc[:],
                                    start=True, stop=True)
            ins.then_inc(s_tot, 1)
            tensor.wait_ge(s_mg, 160)
            tensor.matmul(ps_lg[0:1, 64:80], cntg[:], basemat[:],
                          start=True, stop=True).then_inc(s_mt, 1)
            tensor.wait_ge(s_mid, 1)
            tensor.transpose(ps_pos[0:NT, 96:112], tot_sb[:],
                             ident[0:16, 0:16]).then_inc(s_mt, 1)
            tensor.wait_ge(s_mid, 3)
            tensor.matmul(ps_pos[0:NT, 64:80], lstrict[0:NT, 0:NT], totT_sb[:],
                          start=True, stop=False)
            tensor.matmul(ps_pos[0:NT, 64:80], ones1[0:1, 0:NT], basef[:],
                          start=False, stop=True).then_inc(s_mt, 1)
            tensor.wait_ge(s_mg, 192)
            tensor.wait_ge(s_mid, 4)
            for i in range(NT):
                tensor.matmul(ps_pos[:, i * 16:(i + 1) * 16],
                              ones1[:, 0:128], fns_a[0:1, i * 16:(i + 1) * 16],
                              start=True, stop=False)
                ins = tensor.matmul(ps_pos[:, i * 16:(i + 1) * 16],
                                    lstrict[:], oh_all[:, i * 16:(i + 1) * 16],
                                    start=False, stop=True)
            ins.then_inc(s_fp, 1)

            if lv >= 2:
                tensor.wait_ge(s_ebf, NT)
                tensor.wait_ge(s_wk, 48)
                tensor.wait_ge(s_fv, 1)
            KVB = [ps_m0, ps_m1, ps_m2, ps_m3, ps_pos]
            for tg in range(8 if lv >= 2 else 0):
                for oc in range(8):
                    ktg = tg * 8 + oc
                    if ktg >= 1:
                        tensor.wait_ge(s_akv, ktg)
                    for dch in range(NCH):
                        ins = tensor.matmul(
                            ps_tr[:, 0:512],
                            ew2_sb[:, (dch * 16 + oc) * 128:(dch * 16 + oc + 1) * 128],
                            xn_fm[:, dch * TPC + tg * 512:dch * TPC + (tg + 1) * 512],
                            start=(dch == 0), stop=(dch == NCH - 1))
                    ins.then_inc(s_akm, 1)
                for tt in range(4):
                    gtt = tg * 4 + tt
                    for vh in range(2):
                        idx_v = gtt * 2 + vh
                        if idx_v >= 1:
                            tensor.wait_ge(s_avv, idx_v)
                        for dch in range(NCH):
                            ins = tensor.matmul(
                                ps_tr2[:, 0:512],
                                xn_fm[:, dch * TPC + gtt * 128:dch * TPC + (gtt + 1) * 128],
                                ew2_sb[:, (dch * 16 + 8 + 4 * vh) * 128:
                                       (dch * 16 + 12 + 4 * vh) * 128],
                                start=(dch == 0), stop=(dch == NCH - 1))
                        ins.then_inc(s_avm, 1)
                for tt in range(4):
                    gtt = tg * 4 + tt
                    for h in range(16):
                        kp = gtt * 16 + h
                        tensor.wait_ge(s_akv, tg * 8 + h // 2 + 1)
                        if kp >= 2:
                            tensor.wait_ge(s_akf, kp - 1)
                        tensor.matmul(
                            ps_lg[:, (h % 2) * 256:(h % 2) * 256 + 256],
                            ew2_sb[(h % 2) * 64:(h % 2) * 64 + 64,
                                   KSB0 + (h // 2) * 512 + tt * 128:
                                   KSB0 + (h // 2) * 512 + (tt + 1) * 128],
                            ew2_sb[(h % 2) * 64:(h % 2) * 64 + 64,
                                   PRJ0 + h * 256:PRJ0 + (h + 1) * 256],
                            start=True, stop=True).then_inc(s_akp, 1)
                        tensor.wait_ge(s_akf, kp + 1)
                        tensor.wait_ge(s_avv, gtt * 2 + 2)
                        if gtt >= 1 and h == 0:
                            tensor.wait_ge(s_kva, gtt)
                        for fh in range(2):
                            j = h * 2 + fh
                            tensor.matmul(
                                KVB[j // 8][:, (j % 8) * 64:(j % 8) * 64 + 64],
                                ew2_sb[:, KFB0 + (kp % 2) * 256 + fh * 128:
                                       KFB0 + (kp % 2) * 256 + (fh + 1) * 128],
                                ew2_sb[:, V650 + (gtt % 4) * 1040 + h * 65:
                                       V650 + (gtt % 4) * 1040 + h * 65 + 64],
                                start=True, stop=True)
                            ins = tensor.matmul(
                                ps_pos[:, j:j + 1],
                                ew2_sb[:, KFB0 + (kp % 2) * 256 + fh * 128:
                                       KFB0 + (kp % 2) * 256 + (fh + 1) * 128],
                                ew2_sb[:, V650 + (gtt % 4) * 1040 + h * 65 + 64:
                                       V650 + (gtt % 4) * 1040 + h * 65 + 65],
                                start=True, stop=True)
                            ins.then_inc(s_akk, 1)

            tensor.wait_ge(s_ew, 32)
            for st in range(NST):
                g = st // 2
                tensor.wait_ge(s_cst, st + 1)
                if st >= 1:
                    tensor.wait_ge(s_evB, st)
                pb = ps_tr[:].bitcast(BF16)
                for ch in range(NCH):
                    ins = tensor.transpose(
                        bass.AP(pb.tensor, pb.offset + ch * 128,
                                [[pb.ap[0][0], 128], [1, 128]]),
                        xgb[:, (st % 2) * D + ch * 128:(st % 2) * D + (ch + 1) * 128],
                        identb[:])
                ins.then_inc(s_trB, 1)
                if st % 2 == 1:
                    tensor.wait_ge(s_evB, st + 1)
                    dsp0 = disp_fm.ap().offset + (g % 2) * 2048
                    for hch in range(32):
                        if g * 32 + hch >= 2:
                            tensor.wait_ge(s_hev, g * 32 + hch - 1)
                        dst = ps_lg if hch % 2 == 0 else ps_pos
                        for dch in range(NCH):
                            ins = tensor.matmul(
                                dst[:, 0:256],
                                ew1_sb[:, dch * 4096 + hch * 128:
                                       dch * 4096 + (hch + 1) * 128],
                                bass.AP(disp_fm.ap().tensor, dsp0 + dch * 256,
                                        [[disp_fm.ap().ap[0][0], 128], [1, 256]]),
                                start=(dch == 0), stop=(dch == NCH - 1))
                        ins.then_inc(s_hm, 1)
                    for pss in range(2):
                        if g * 2 + pss >= 1:
                            tensor.wait_ge(s_yev, g * 2 + pss)
                        for hch in range(32):
                            tensor.wait_ge(s_hev, g * 32 + hch + 1)
                            for dq in range(4):
                                dch = pss * 4 + dq
                                dstb = [ps_m0, ps_m1, ps_m2, ps_m3][dq]
                                ins = tensor.matmul(
                                    dstb[:, 0:256],
                                    ew2_sb[:, hch * D + dch * 128:
                                           hch * D + (dch + 1) * 128],
                                    h_fm[:, hch * 256:(hch + 1) * 256],
                                    start=(hch == 0), stop=(hch == 31))
                        ins.then_inc(s_ya, 1)
                    for q in range(2):
                        st2 = g * 2 + q
                        tensor.wait_ge(s_yev, g * 2 + 2)
                        if st2 >= 1:
                            tensor.wait_ge(s_ytv, st2)
                        pb2 = ps_tr2[:].bitcast(BF16)
                        for dch in range(NCH):
                            ins = tensor.transpose(
                                bass.AP(pb2.tensor, pb2.offset + dch * 128,
                                        [[pb2.ap[0][0], 128], [1, 128]]),
                                y_fm[:, dch * 256 + q * 128:dch * 256 + (q + 1) * 128],
                                identb[:])
                        ins.then_inc(s_ytr, 1)


            if lv >= 9:
                tensor.wait_ge(s_kvl, 16)
                tensor.wait_ge(s_wq2, 32)
                tensor.wait_ge(s_kvb, 1)
                KVBF = 4160
                for tg in range(8):
                    tgp = tg % 2
                    if tg >= 1:
                        tensor.wait_ge(s_a4v, B_a4v + 4 * tg)
                        tensor.wait_ge(s_ax4, tg)
                    for tt in range(4):
                        gt = tg * 4 + tt
                        tensor.wait_ge(s_axn, B_axn + gt + 1)
                        if gt >= 1:
                            tensor.wait_ge(s_aev, B_aev + 2 * gt)
                        for h2 in range(2):
                            dst = ps_tr if h2 == 0 else ps_tr2
                            for q in range(4):
                                ch = h2 * 4 + q
                                ins = tensor.transpose(
                                    dst[:, q * 128:(q + 1) * 128],
                                    xn_f32[:, (gt % 2) * D + ch * 128:
                                           (gt % 2) * D + (ch + 1) * 128],
                                    ident[:])
                            ins.then_inc(s_atr, 1)
                    for qc in range(8):
                        u = tg * 8 + qc
                        tensor.wait_ge(s_awq[u % 2], 16 * (u // 2 + 1))
                        tensor.wait_ge(s_aev, B_aev + 8 * (tg + 1))
                        if u >= 1:
                            tensor.wait_ge(s_aqv, B_aqv + u)
                        for dch in range(NCH):
                            ins = tensor.matmul(
                                ps_m2[:, 0:512],
                                ew2_sb[:, WQB0 + (u % 2) * 1024 + dch * 128:
                                       WQB0 + (u % 2) * 1024 + (dch + 1) * 128],
                                disp_fm[:, dch * 512:(dch + 1) * 512],
                                start=(dch == 0), stop=(dch == NCH - 1))
                        ins.then_inc(s_aqm, 1)
                        if qc % 2 == 0 and u >= 2:
                            tensor.wait_ge(s_aov, B_aov + u - 1)
                        for hq in range(2):
                            h = qc * 2 + hq
                            hh = tg * 16 + h
                            tensor.wait_ge(s_aqv, B_aqv + u + 1)
                            for fh in range(2):
                                c_qf = hh * 2 + fh
                                if c_qf >= 1:
                                    tensor.wait_ge(s_aqe, B_aqe + c_qf)
                                tensor.matmul(
                                    ps_m3[:, 0:512],
                                    ew2_sb[hq * 64:hq * 64 + 64,
                                           PRJ2 + h * 256 + fh * 128:
                                           PRJ2 + h * 256 + (fh + 1) * 128],
                                    ew1_sb[hq * 64:hq * 64 + 64,
                                           QSC + (u % 2) * 512:QSC + (u % 2 + 1) * 512],
                                    start=True, stop=True).then_inc(s_aqf, 1)
                            tensor.wait_ge(s_aqe, B_aqe + 2 * hh + 2)
                            for fh in range(2):
                                j = h * 2 + fh
                                ins = tensor.matmul(
                                    ps_lg[0:1, 0:512],
                                    h_fm[:, KVBF + 2048 + j:KVBF + 2048 + j + 1],
                                    ew1_sb[:, QFC + fh * 512:QFC + (fh + 1) * 512],
                                    start=(fh == 0), stop=(fh == 1))
                            ins.then_inc(s_azm, 1)
                            tensor.wait_ge(s_azv, B_azv + hh + 1)
                            tensor.matmul(ps_lg[:, 0:512], ones1[:, 0:128],
                                          y_tm[:].bitcast(F32)[0:1, 0:512],
                                          start=True, stop=True).then_inc(s_azb, 1)
                            tensor.wait_ge(s_aqz, B_aqz + hh + 1)
                            bank = ps_m0 if qc % 2 == 0 else ps_m1
                            for fh in range(2):
                                j = h * 2 + fh
                                ins = tensor.matmul(
                                    bank[hq * 64:hq * 64 + 64, 0:512],
                                    h_fm[:, KVBF + j * 64:KVBF + (j + 1) * 64],
                                    ew1_sb[:, QFC + fh * 512:QFC + (fh + 1) * 512],
                                    start=(fh == 0), stop=(fh == 1))
                            ins.then_inc(s_aam, 1)
                    for dc in range(8):
                        u = tg * 8 + dc
                        tensor.wait_ge(s_aov, B_aov + tg * 8 + 8)
                        tensor.wait_ge(s_awo[0], B_awo + 16 * (u + 1))
                        if u >= 2:
                            tensor.wait_ge(s_axv, B_axv + u - 1)
                        bank = ps_m2 if dc % 2 == 0 else ps_m3
                        for oc in range(8):
                            ins = tensor.matmul(
                                bank[:, 0:512],
                                ew2_sb[:, WOB0 + (u % 2) * 1024 + oc * 128:
                                       WOB0 + (u % 2) * 1024 + (oc + 1) * 128],
                                ew1_sb[:, AOC + oc * 512:AOC + (oc + 1) * 512],
                                start=(oc == 0), stop=(oc == 7))
                        ins.then_inc(s_axm, 1)
                    for r in range(2):
                        XS = ew1_sb[:, 0:8192].bitcast(F32)
                        SQ = ew1_sb[:, 8192:16384].bitcast(F32)
                        if r == 0:
                            tensor.wait_ge(s_axv, B_axv + tg * 8 + 8)
                        else:
                            tensor.wait_ge(s_axe, B_axe + tg * 2 + 2)
                        tensor.wait_ge(s_asq, B_asq + tg * 16 + r * 8 + 8)
                        for tt in range(4):
                            for dch in range(NCH):
                                ins = tensor.matmul(
                                    ps_lg[:, 256 + tt:257 + tt], 
                                    bass.AP(XS.tensor, XS.offset + dch * 512 + tt * 128,
                                            [[XS.ap[0][0], 128], [1, 128]]),
                                    onesc[:], start=(dch == 0), stop=(dch == NCH - 1))
                            ins.then_inc(s_asm, 0) if False else None
                            for dch in range(NCH):
                                ins = tensor.matmul(
                                    ps_lg[:, 260 + tt:261 + tt],
                                    bass.AP(SQ.tensor, SQ.offset + dch * 512 + tt * 128,
                                            [[SQ.ap[0][0], 128], [1, 128]]),
                                    onesc[:], start=(dch == 0), stop=(dch == NCH - 1))
                        ins.then_inc(s_asm, 1)
                        tensor.wait_ge(s_ars, B_ars + tg * 2 + r + 1)
                        for tt in range(4):
                            tensor.transpose(ps_pos[0:1, tt * 128:(tt + 1) * 128],
                                             stt[:, tt:tt + 1], ident[:])
                            ins = tensor.transpose(ps_m2[0:1, tt * 128:(tt + 1) * 128],
                                                   stt[:, 8 + tt:9 + tt], ident[:])
                        ins.then_inc(s_arw, 1)
                        tensor.wait_ge(s_ast, B_ast + (tg * 2 + r) * 2 + 2)
                        tensor.matmul(ps_pos[:, 0:512], ones1[:, 0:128],
                                      xn_f32[0:1, 0:512], start=True, stop=True)
                        ins = tensor.matmul(ps_m2[:, 0:512], ones1[:, 0:128],
                                            xn_f32[0:1, 512:1024], start=True, stop=True)
                        ins.then_inc(s_arw, 1)
                        if r == 0:
                            tensor.wait_ge(s_a2b, B_a2b + tg * 8 + 8)
                            for hch in range(32):
                                uu = tg * 32 + hch
                                tensor.wait_ge(s_aw1[0], B_aw1 + 16 * (uu + 1))
                                if uu >= 2:
                                    tensor.wait_ge(s_ahe, B_ahe + uu - 1)
                                dsth = ps_lg if hch % 2 == 0 else ps_pos
                                for dch in range(NCH):
                                    ins = tensor.matmul(
                                        dsth[:, 0:512],
                                        ew2_sb[:, W1B0 + (uu % 2) * 1024 + dch * 128:
                                               W1B0 + (uu % 2) * 1024 + (dch + 1) * 128],
                                        ew1_sb[:, X2BC + dch * 512:X2BC + (dch + 1) * 512],
                                        start=(dch == 0), stop=(dch == NCH - 1))
                                ins.then_inc(s_ahm, 1)
                            for pss in range(2):
                                for hch in range(32):
                                    uu = tg * 64 + pss * 32 + hch
                                    tensor.wait_ge(s_ahe, B_ahe + tg * 32 + hch + 1)
                                    tensor.wait_ge(s_aw2[uu % 2], 16 * (uu // 2 + 1))
                                    if pss == 0 and hch == 0 and tg >= 1:
                                        tensor.wait_ge(s_axe, B_axe + tg * 2)
                                    if pss == 1 and hch == 0:
                                        tensor.wait_ge(s_axe, B_axe + tg * 2 + 1)
                                    for dq in range(4):
                                        dch = pss * 4 + dq
                                        dstb = [ps_m0, ps_m1, ps_m2, ps_m3][dq]
                                        ins = tensor.matmul(
                                            dstb[:, 0:512],
                                            ew2_sb[:, W2B0 + (uu % 2) * 1024 + dch * 128:
                                                   W2B0 + (uu % 2) * 1024 + (dch + 1) * 128],
                                            ew2_sb[:, H2A0 + hch * 512:H2A0 + (hch + 1) * 512],
                                            start=(hch == 0), stop=(hch == 31))
                                    ins.then_inc(s_aw2c, 1)
                    tensor.wait_ge(s_ax4, B_ax4 + tg + 1)
                    for tt in range(4):
                        gt = tg * 4 + tt
                        if gt >= 1:
                            tensor.wait_ge(s_a4v, B_a4v + gt)
                        X4 = ew1_sb[:, 8192:16384].bitcast(F32)
                        for h2 in range(2):
                            dst = ps_tr if h2 == 0 else ps_tr2
                            for q in range(4):
                                dc = h2 * 4 + q
                                ins = tensor.transpose(
                                    dst[:, q * 128:(q + 1) * 128],
                                    bass.AP(X4.tensor, X4.offset + dc * 512 + tt * 128,
                                            [[X4.ap[0][0], 128], [1, 128]]),
                                    ident[:])
                            ins.then_inc(s_a4t, 1)

        @block.gpsimd
        def _(gpsimd):
            bc_slot = gpsimd.alloc_register()
            gpsimd.reg_mov(bc_slot, NSLOT - 1)
            wkv_src = bass.AP(wqkv_l.ap().tensor, wqkv_l.ap().offset + 8 * 128,
                              [[24 * 128 * 8, 128], [24 * 128, NCH], [1, 16 * 128]])
            gpsimd.dma_start(out=ew2_sb[:, 0:16384], in_=wkv_src).then_inc(s_wk, 16)
            gpsimd.dma_start(out=ew2_sb[0:64, PRJ0:PRJ0 + 4096],
                             in_=proj_l[:, :]).then_inc(s_wk, 16)
            gpsimd.dma_start(out=ew2_sb[64:128, PRJ0:PRJ0 + 4096],
                             in_=proj_l[:, :]).then_inc(s_wk, 16)
            wq_src = bass.AP(wqkv_l.ap().tensor, wqkv_l.ap().offset,
                             [[24 * 128 * 8, 128], [24 * 128, NCH], [1, 8 * 128]])
            gpsimd.dma_start(out=wq_bf[:, :], in_=wq_src).then_inc(s_wk, 16)
            gpsimd.dma_start(out=wout_bf[:, :], in_=wout_l[:, :]).then_inc(s_wk, 16)
            gpsimd.dma_start(out=w1a_bf[:, :], in_=w1a_l[:, :]).then_inc(s_wk, 16)
            gpsimd.dma_start(out=w2a_bf[:, :], in_=w2a_l[:, :]).then_inc(s_wk, 16)
            bc_tok = gpsimd.alloc_register()
            gpsimd.reg_mov(bc_tok, NTOK - 1)
            bc_yall = gpsimd.alloc_register()
            gpsimd.reg_mov(bc_yall, NSLOT - 1)
            gpsimd.memset(myrec[:, 0:256], 0.0)
            gpsimd.drain()
            rcv = rec_c.ap().rearrange("(a p c2) c -> a p (c2 c)", a=8, p=128)
            for a in range(8):
                gpsimd.dma_start(out=rcv[a], in_=myrec[:, 0:256]).then_inc(s_mg, 16)
            gpsimd.wait_ge(s_mid, 1)
            gpsimd.dma_start(out=cnt_in[:, :], in_=cnt_sb[:]).then_inc(s_mg, 16)
            gpsimd.wait_ge(s_mg, 144)
            gpsimd.collective_compute(
                "AllGather", ALU.bypass, replica_groups=RG,
                ins=[cnt_in.ap().opt()], outs=[cnt_out.ap().opt()],
            ).then_inc(s_cc, 1)
            gpsimd.wait_ge(s_cc, 1)
            gpsimd.dma_start(out=cntg[:], in_=cnt_out[:, :]).then_inc(s_mg, 16)
            gpsimd.wait_ge(s_mid, 4)
            gpsimd.dma_start(
                out=runrow_d.ap().rearrange("o (a b) -> (o a) b", a=NT, b=16),
                in_=runp_sb[:]).then_inc(s_mg, 16)
            gpsimd.wait_ge(s_mg, 176)
            gpsimd.dma_start(out=fns_a[0:1, 0:NT * 16], in_=runrow_d[0:1, :]).then_inc(s_mg, 16)
            gpsimd.wait_ge(s_fv, 1)
            for i in range(NT):
                for k in range(2):
                    gpsimd.indirect_dma_start(
                        out=rec_c[:, :],
                        out_offset=IndirectOffsetOnAxis(
                            ap=sidx_a[:, i * 2 + k:i * 2 + k + 1], axis=0),
                        in_=lg_all[:, i * 8 + 4 * k:i * 8 + 4 * k + 4],
                        in_offset=None,
                        bounds_check=bc_slot, oob_is_err=False,
                    ).then_inc(s_sc, 16)
            gpsimd.wait_ge(s_sc, 16 * 2 * NT)
            gpsimd.collective_compute(
                "AllReduce", ALU.add, replica_groups=RG,
                ins=[rec_c.ap().opt()], outs=[rec_all.ap().opt()],
            ).then_inc(s_cc, 1)
            gpsimd.wait_ge(s_cc, 2)
            for sti in range(NST):
                gpsimd.indirect_dma_start(
                    out=myrec[:, sti * 4:(sti + 1) * 4],
                    out_offset=None,
                    in_=rec_all[:, :],
                    in_offset=IndirectOffsetOnAxis(
                        ap=gidxc[:, sti:sti + 1], axis=0),
                    bounds_check=None, oob_is_err=False,
                ).then_inc(s_gr, 16)
            if lv >= 2:
                gpsimd.wait_ge(s_akm, 8 * 8)
                gpsimd.wait_ge(s_avm, 8 * 8)
                gpsimd.wait_ge(s_akk, 2 * 16 * NT)
                gpsimd.wait_ge(s_akvd, 1)
            else:
                gpsimd.wait_ge(s_ebf, NT)
            gpsimd.dma_start(out=ew1_sb[:], in_=ew1[:, :]).then_inc(s_ew, 16)
            gpsimd.dma_start(out=ew2_sb[:], in_=ew2[:, :]).then_inc(s_ew, 16)
            gpsimd.wait_ge(s_tix, 1)
            for st in range(NST):
                if st >= 2:
                    gpsimd.wait_ge(s_cst, st - 1)
                gpsimd.indirect_dma_start(
                    out=xgl[:, (st % 2) * D:(st % 2 + 1) * D],
                    out_offset=None,
                    in_=xf[:, :],
                    in_offset=IndirectOffsetOnAxis(
                        ap=tidx_a[:, st:st + 1], axis=0),
                    bounds_check=bc_tok, oob_is_err=False,
                ).then_inc(s_xg[st % 2], 16)
            if lv >= 2:
                gpsimd.wait_ge(s_kvin, 16)
                gpsimd.collective_compute(
                    "AllReduce", ALU.add, replica_groups=RGP,
                    ins=[kv_in.ap().opt()], outs=[kv_out.ap().opt()],
                ).then_inc(s_cc, 1)
            if lv >= 9:
                gpsimd.wait_ge(s_ya, 64)
                gpsimd.wait_ge(s_cc, 3)
                gpsimd.dma_start(out=h_fm[:, 0:4160].bitcast(F32),
                                 in_=kv_out[:, :]).then_inc(s_kvl, 16)
                gpsimd.dma_start(out=ew2_sb[0:64, PRJ2:PRJ2 + 4096],
                                 in_=proj_l[:, :]).then_inc(s_wq2, 16)
                gpsimd.dma_start(out=ew2_sb[64:128, PRJ2:PRJ2 + 4096],
                                 in_=proj_l[:, :]).then_inc(s_wq2, 16)
            gpsimd.wait_ge(s_ym[0], 16 * (NST // 2))
            gpsimd.wait_ge(s_ym[1], 16 * (NST // 2))
            gpsimd.collective_compute(
                "AllGather", ALU.bypass, replica_groups=RG,
                ins=[y_mine.ap().opt()], outs=[y_all.ap().opt()],
            ).then_inc(s_cc, 1)
            gpsimd.wait_ge(s_cc, CC_YAG)
            gpsimd.wait_ge(s_fv, 1)
            for i in range(NT):
                if i >= 2:
                    gpsimd.wait_ge(s_cmb, i - 1)
                    if lv >= 9:
                        gpsimd.wait_ge(s_o1[(i - 2) % 2], 16 * ((i - 2) // 2 + 1))
                        gpsimd.dma_start(out=out_d[(i - 2) * 128:(i - 1) * 128, :],
                                         in_=y_tm[:, 0:D],
                                         accum_op=ALU.add).then_inc(s_moA, 16)
                gpsimd.indirect_dma_start(
                    out=y_fm[:, (i % 2) * D:(i % 2 + 1) * D],
                    out_offset=None,
                    in_=y_all[:, :],
                    in_offset=IndirectOffsetOnAxis(
                        ap=yidx_a[:, 2 * i:2 * i + 1], axis=0),
                    bounds_check=bc_yall, oob_is_err=False,
                ).then_inc(s_yg[i % 2], 16)
                gpsimd.indirect_dma_start(
                    out=xgb[:, (i % 2) * D:(i % 2 + 1) * D],
                    out_offset=None,
                    in_=y_all[:, :],
                    in_offset=IndirectOffsetOnAxis(
                        ap=yidx_a[:, 2 * i + 1:2 * i + 2], axis=0),
                    bounds_check=bc_yall, oob_is_err=False,
                ).then_inc(s_yg[i % 2], 16)
            for i in ([NT - 2, NT - 1] if lv >= 9 else []):
                gpsimd.wait_ge(s_cmb, i + 1)
                gpsimd.wait_ge(s_o1[i % 2], 16 * (i // 2 + 1))
                gpsimd.dma_start(out=out_d[i * 128:(i + 1) * 128, :],
                                 in_=y_tm[:, 0:D],
                                 accum_op=ALU.add).then_inc(s_moA, 16)

    return nc



def _build_inmaps(inputs):
    import ml_dtypes
    x = np.ascontiguousarray(
        np.asarray(inputs["L0_attn_x"], np.float32).reshape(-1, D))
    wg0 = np.asarray(inputs["L0_ffn_wg"], np.float32)
    wg = np.ascontiguousarray(
        wg0.reshape(NCH, 128, E).transpose(1, 0, 2).reshape(128, NCH * E))
    w_qkv = np.asarray(inputs["L0_attn_w_qkv"], np.float32)
    proj = np.asarray(inputs["L0_attn_proj"], np.float32)
    w_out = np.asarray(inputs["L0_attn_w_out"], np.float32)
    b_out = np.asarray(inputs["L0_attn_b_out"], np.float32)
    gamma = np.asarray(inputs["L0_attn_gamma"], np.float32)
    beta = np.asarray(inputs["L0_attn_beta"], np.float32)
    w1a = np.asarray(inputs["L0_attn_w1"], np.float32)
    b1a = np.asarray(inputs["L0_attn_b1"], np.float32)
    w2a = np.asarray(inputs["L0_attn_w2"], np.float32)
    b2a = np.asarray(inputs["L0_attn_b2"], np.float32)
    fw1 = np.asarray(inputs["L0_ffn_w1"], np.float32)
    fb1 = np.asarray(inputs["L0_ffn_b1"], np.float32)
    fw2 = np.asarray(inputs["L0_ffn_w2"], np.float32)
    fb2 = np.asarray(inputs["L0_ffn_b2"], np.float32)

    wqkv_l = np.ascontiguousarray(
        w_qkv.reshape(24, 128, 8, 128).transpose(3, 2, 0, 1).reshape(128, 24576))
    proj_l = np.ascontiguousarray(proj.transpose(1, 0, 2).reshape(64, 4096))
    wout_l = np.ascontiguousarray(
        w_out.reshape(8, 128, 8, 128).transpose(3, 0, 2, 1).reshape(128, 8192))
    w1a_l = np.ascontiguousarray(
        w1a.reshape(32, 128, 8, 128).transpose(3, 0, 2, 1).reshape(128, 32768))
    w2a_l = np.ascontiguousarray(
        w2a.reshape(8, 128, 32, 128).transpose(3, 2, 0, 1).reshape(128, 32768))

    in_maps = []
    for c in range(8):
        m = dict(xf=x, xm=x[c * TPC:(c + 1) * TPC], wg=wg,
                 wqkv_l=wqkv_l, proj_l=proj_l, wout_l=wout_l,
                 w1a_l=w1a_l, w2a_l=w2a_l,
                 bout_c=np.ascontiguousarray(b_out.reshape(8, 128).T),
                 gam_c=np.ascontiguousarray(gamma.reshape(8, 128).T),
                 bet_c=np.ascontiguousarray(beta.reshape(8, 128).T),
                 b1a_c=np.ascontiguousarray(b1a.reshape(32, 128).T),
                 b2a_c=np.ascontiguousarray(b2a.reshape(8, 128).T))
        m["ew1"] = np.ascontiguousarray(
            fw1[c].reshape(NCH, 128, 4096).transpose(1, 0, 2).reshape(128, NCH * 4096))
        m["ew2"] = np.ascontiguousarray(
            fw2[c].reshape(32, 128, D).transpose(1, 0, 2).reshape(128, 32 * D))
        m["eb1_i"] = np.ascontiguousarray(fb1[c].reshape(32, 128).T)
        m["eb2_i"] = np.ascontiguousarray(fb2[c].reshape(8, 128).T)
        m["identb_i"] = np.eye(128).astype(ml_dtypes.bfloat16)
        m.update(make_const_inputs(c))
        in_maps.append(m)
    return in_maps


_NC_CACHE = {}


def kernel(**inputs):
    from concourse.bass_utils import run_bass_kernel_spmd
    if "C" not in _NC_CACHE:
        _NC_CACHE["C"] = build("C")
    nc = _NC_CACHE["C"]
    in_maps = _build_inmaps(inputs)
    res = run_bass_kernel_spmd(nc, in_maps, core_ids=list(range(8)))
    out = np.concatenate(
        [np.asarray(res.results[c]["out_d"], np.float32) for c in range(8)], axis=0)
    return out.reshape(4, 8192, D)
